# version 03
# baseline (speedup 1.0000x reference)
import math
from contextlib import ExitStack

import numpy as np
import ml_dtypes
import concourse.bass as bass
import concourse.mybir as mybir
from concourse.alu_op_type import AluOpType as OP
from concourse.bass_utils import run_bass_kernel_spmd

F32 = mybir.dt.float32
BF16 = mybir.dt.bfloat16
I32 = mybir.dt.int32
AF = mybir.ActivationFunctionType
AX = mybir.AxisListType

S, DM, H, HD = 2048, 4096, 32, 128
NT = S // 128
KC = DM // 128
NE, RANK, TOPK, RDIM = 64, 8, 16, 16
BOT = NE * RANK
FFN_SCALE = 2.0
NEGBIG = -1.0e9
EPS = 1e-5
SLOPES = [2.0 ** (-(i + 1) * 8.0 / 32.0) for i in range(H)]
Q_SCALE = 1.0 / math.sqrt(HD)

_CACHE = {}
STAGE = 'full'


class KB:
    ENGS = ("tensor", "vector", "scalar", "gpsimd", "sync")

    def __init__(self, nc, ctx):
        self.nc = nc
        self.ctx = ctx
        self.q = {e: [] for e in self.ENGS}
        self.sems = {}
        self.cnt = {}

    def sem(self, name):
        if name not in self.sems:
            self.sems[name] = self.ctx.enter_context(self.nc.semaphore(name))
            self.cnt[name] = 0
        return name

    def emit(self, eng, fn, inc=None, n=1):
        cell = []

        def runner(e, fn=fn, cell=cell):
            inst = fn(e)
            for (sh, k) in cell:
                inst.then_inc(sh, k)
            return inst
        self.q[eng].append(runner)
        self.last_cell = getattr(self, "last_cell", {})
        self.last_cell[eng] = cell
        if inc is None:
            return None
        cell.append((self.sems[inc], n))
        self.cnt[inc] += n
        return self.cnt[inc]

    def mark(self, eng, sem_name, n=1):
        self.last_cell[eng].append((self.sems[sem_name], n))
        self.cnt[sem_name] += n
        return self.cnt[sem_name]

    def wait(self, eng, sem_name, val):
        if val <= 0:
            return
        s = self.sems[sem_name]
        self.q[eng].append(lambda e, s=s, v=val: e.wait_ge(s, v))

    def barrier(self, engines=None):
        engines = engines or self.ENGS
        self.sem("bar")
        tgt = self.cnt["bar"] + len(engines)
        for e in engines:
            if e == "vector":
                self.emit(e, lambda v: v.memset(self.dum_v[:], 0.0), inc="bar")
            elif e == "scalar":
                self.emit(e, lambda x: x.copy(self.dum_s[:], self.dum_v[:]), inc="bar")
            elif e == "gpsimd":
                self.emit(e, lambda g: g.memset(self.dum_g[:], 0.0), inc="bar")
            else:
                self.emit(e, lambda x: x.nop(), inc="bar")
        for e in engines:
            self.wait(e, "bar", tgt)

    def dma(self, qeng, dst, src, sem):
        return self.emit(qeng, lambda e, d=dst, s=src: e.dma_start(d, s), inc=sem, n=16)


def build_program(stage, debug):
    lvl = {'0a': 0, '0b': 1, '0c': 2, 'A': 3, 'B': 4, 'C': 5, 'full': 6}[stage]
    nc = bass.Bass("TRN2", target_bir_lowering=False)

    def dram(name, shape, dtype, kind):
        if kind == "Internal" and name in debug:
            kind = "ExternalOutput"
        return nc.dram_tensor(name, shape, dtype, kind=kind).ap()

    x0 = dram("x0", [S, DM], F32, "ExternalInput")
    mask_i = dram("mask_i", [DM, DM], I32, "ExternalInput")
    waT = dram("waT", [DM, DM], F32, "ExternalInput")
    bias_a = dram("bias_a", [1, DM], F32, "ExternalInput")
    wqkvT = dram("wqkvT", [DM, 3 * DM], F32, "ExternalInput")
    bqkv = dram("bqkv", [1, 3 * DM], F32, "ExternalInput")
    woutT = dram("woutT", [DM, DM], F32, "ExternalInput")
    bout = dram("bout", [1, DM], F32, "ExternalInput")
    AT = dram("AT", [DM, BOT], F32, "ExternalInput")
    BT = dram("BT", [BOT, DM], F32, "ExternalInput")
    WqT = dram("WqT", [DM, RDIM], F32, "ExternalInput")
    keysT = dram("keysT", [RDIM, NE], F32, "ExternalInput")
    id_bf = dram("id_bf", [128, 128], BF16, "ExternalInput")
    id_f = dram("id_f", [128, 128], F32, "ExternalInput")
    causal_c = dram("causal_c", [128, 128], F32, "ExternalInput")
    pos_c = dram("pos_c", [128, S], F32, "ExternalInput")
    alibi_c = dram("alibi_c", [H, S], F32, "ExternalInput")
    blk_ind = dram("blk_ind", [128, DM], BF16, "ExternalInput")
    blk_indT = dram("blk_indT", [DM, 128], BF16, "ExternalInput")
    repE = dram("repE", [NE, BOT], BF16, "ExternalInput")

    y = dram("y", [S, DM], F32, "ExternalOutput")
    wm_blk = dram("wm_blk", [KC, 8, 128, 512], BF16, "Internal")
    h1T_sc = dram("h1T_sc", [KC, 128, S], BF16, "Internal")
    hT_sc = dram("hT_sc", [KC, 128, S], BF16, "Internal")
    x1_sc = dram("x1_sc", [S, DM], F32, "Internal")
    x2_sc = dram("x2_sc", [S, DM], F32, "Internal")
    x3_sc = dram("x3_sc", [S, DM], F32, "Internal")
    attnT_sc = dram("attnT_sc", [KC, 128, S], BF16, "Internal")

    with ExitStack() as ctx:
        kb = KB(nc, ctx)
        for s_ in ("pe", "dve", "act", "gp", "qx", "qw", "qst", "bar"):
            kb.sem(s_)

        def sb(name, shape, dtype, cx=None):
            return (cx or ctx).enter_context(nc.sbuf_tensor(name, shape, dtype))

        kb.dum_v = sb("dum_v", [128, 1], F32)
        kb.dum_s = sb("dum_s", [128, 1], F32)
        kb.dum_g = sb("dum_g", [128, 1], F32)
        id_sb = sb("id_sb", [128, 128], BF16)
        idf_sb = sb("idf_sb", [128, 128], F32)
        causal_sb = sb("causal_sb", [128, 128], F32)

        pb = [ctx.enter_context(nc.psum_tensor(f"pb{i}", [128, 512], F32)) for i in range(6)]
        pTb = [ctx.enter_context(nc.psum_tensor(f"pT{i}", [128, 1024], BF16)) for i in range(2)]

        kb.dma("sync", id_sb[:], id_bf, "qx")
        kb.dma("sync", idf_sb[:], id_f, "qx")
        kb.dma("sync", causal_sb[:], causal_c, "qx")
        CONST_QX = kb.cnt["qx"]

        def ln_phase(tag, src_dram, dst_dram=None, dst_sb=None):
            with ExitStack() as cx:
                xf = [sb(f"{tag}xf{i}", [128, DM], F32, cx) for i in range(2)]
                hb = sb(f"{tag}hb", [128, DM], BF16, cx)
                stage_b = None
                if dst_dram is not None:
                    stage_b = [sb(f"{tag}st{i}", [128, KC * 128], BF16, cx) for i in range(2)]
                bn48 = sb(f"{tag}bn", [128, 48], F32, cx)
                mv = sb(f"{tag}mv", [128, 2], F32, cx)
                sd = sb(f"{tag}sd", [128, 1], F32, cx)
                rstd = sb(f"{tag}rs", [128, 1], F32, cx)
                negmr = sb(f"{tag}nm", [128, 1], F32, cx)
                xsem = [kb.sem(f"{tag}x{i}") for i in range(2)]

                act_norm = {}
                evac_done = {0: 0, 1: 0}
                pe_done = {}
                st_done = {0: 0, 1: 0}
                for t in range(NT):
                    sl = t % 2
                    if t >= 2:
                        kb.wait("sync", "act", act_norm[t - 2])
                    kb.dma("sync", xf[sl][:], src_dram[t * 128:(t + 1) * 128, :], xsem[sl])
                    if t >= 1:
                        kb.wait("vector", "act", act_norm[t - 1])
                    kb.wait("vector", xsem[sl], kb.cnt[xsem[sl]])
                    for c in range(8):
                        kb.emit("vector", lambda v, sl=sl, c=c: v.bn_stats(
                            bn48[:, c * 6:(c + 1) * 6],
                            xf[sl][:, c * 512:(c + 1) * 512]), inc="dve")
                    kb.wait("vector", "dve", kb.cnt["dve"])
                    kb.emit("vector", lambda v: v.bn_aggr(mv[:], bn48[:]), inc="dve")
                    kb.wait("vector", "dve", kb.cnt["dve"])
                    kb.emit("vector", lambda v: v.tensor_scalar(
                        mv[:, 1:2], mv[:, 1:2], EPS, None, op0=OP.add), inc="dve")
                    kb.wait("scalar", "dve", kb.cnt["dve"])
                    kb.emit("scalar", lambda s: s.activation(
                        sd[:], mv[:, 1:2], AF.Sqrt), inc="act")
                    kb.wait("vector", "act", kb.cnt["act"])
                    kb.emit("vector", lambda v: v.reciprocal(rstd[:], sd[:]), inc="dve")
                    kb.wait("vector", "dve", kb.cnt["dve"])
                    kb.emit("vector", lambda v: v.scalar_tensor_tensor(
                        negmr[:], mv[:, 0:1], -1.0, rstd[:],
                        op0=OP.mult, op1=OP.mult), inc="dve")
                    kb.wait("scalar", "dve", kb.cnt["dve"])
                    if t >= 1:
                        kb.wait("scalar", "pe", pe_done[t - 1])
                    kb.emit("scalar", lambda s, sl=sl: s.activation(
                        hb[:], xf[sl][:], AF.Identity,
                        bias=negmr[:], scale=rstd[:]), inc="act")
                    act_norm[t] = kb.cnt["act"]
                    kb.wait("tensor", "act", act_norm[t])
                    for grp in range(8):
                        bk = grp % 2
                        kb.wait("tensor", "act", evac_done[bk])
                        for j in range(4):
                            kc = grp * 4 + j
                            kb.emit("tensor", lambda e, kc=kc, j=j, bk=bk: e.transpose(
                                pTb[bk][:, j * 128:(j + 1) * 128],
                                hb[:, kc * 128:(kc + 1) * 128], id_sb[:]),
                                inc=("pe" if j == 3 else None))
                        kb.wait("scalar", "pe", kb.cnt["pe"])
                        if dst_sb is not None:
                            kb.emit("scalar", lambda s, grp=grp, bk=bk, t=t: s.copy(
                                dst_sb[:, grp * 4:(grp + 1) * 4, t * 128:(t + 1) * 128],
                                pTb[bk][:, 0:512]
                                .rearrange("p (c f) -> p c f", f=128)), inc="act")
                        else:
                            if grp == 0:
                                kb.wait("scalar", "qst", st_done[sl])
                            kb.emit("scalar", lambda s, grp=grp, bk=bk, sl=sl: s.copy(
                                stage_b[sl][:, grp * 512:(grp + 1) * 512],
                                pTb[bk][:, 0:512]), inc="act")
                        evac_done[bk] = kb.cnt["act"]
                    pe_done[t] = kb.cnt["pe"]
                    if dst_dram is not None:
                        kb.wait("sync", "act", kb.cnt["act"])
                        kb.dma("sync",
                               dst_dram[:, :, t * 128:(t + 1) * 128]
                               .rearrange("c k f -> k c f"),
                               stage_b[sl][:].rearrange("k (c f) -> k c f", c=KC), "qst")
                        st_done[sl] = kb.cnt["qst"]
                kb.wait("sync", "qst", kb.cnt["qst"])
            kb.barrier()

        bmT_b = sb("bmT_b", [128, 128], BF16)
        with ExitStack() as cx:
            mkf = [sb(f"mkf{i}", [128, DM], F32, cx) for i in range(2)]
            t1 = [sb(f"t1_{i}", [128, 128], BF16, cx) for i in range(2)]
            t1f = sb("t1f", [128, 128], F32, cx)
            indT_sb = sb("indT_sb", [128, KC * 128], BF16, cx)
            msem = [kb.sem(f"mk{i}") for i in range(2)]

            kb.dma("gpsimd", indT_sb[:].rearrange("p (c i) -> p c i", c=KC),
                   blk_indT.rearrange("(c p) i -> p c i", p=128), "qw")
            IND_QW = kb.cnt["qw"]
            kb.wait("tensor", "qw", IND_QW)
            t1_done = {0: 0, 1: 0}
            mm_done = {0: 0, 1: 0}
            for c in range(KC):
                sl = c % 2
                kb.wait("gpsimd", "dve", t1_done[sl])
                kb.dma("gpsimd", mkf[sl][:], mask_i[c * 128:(c + 1) * 128, :], msem[sl])
                kb.wait("vector", msem[sl], kb.cnt[msem[sl]])
                kb.wait("vector", "pe", mm_done[sl])
                kb.emit("vector", lambda v, sl=sl: v.tensor_reduce(
                    t1f[:], mkf[sl][:].rearrange("p (j w) -> p j w", w=32),
                    axis=AX.X, op=OP.add), inc="dve")
                kb.wait("vector", "dve", kb.cnt["dve"])
                kb.emit("vector", lambda v, sl=sl: v.tensor_copy(
                    t1[sl][:], t1f[:]), inc="dve")
                t1_done[sl] = kb.cnt["dve"]
                kb.wait("tensor", "dve", t1_done[sl])
                mm_done[sl] = kb.emit("tensor", lambda e, sl=sl, c=c: e.matmul(
                    pb[5][:, 0:128], t1[sl][:], indT_sb[:, c * 128:(c + 1) * 128],
                    start=(c == 0), stop=(c == KC - 1)), inc="pe")
            kb.wait("vector", "pe", kb.cnt["pe"])
            kb.emit("vector", lambda v: v.tensor_scalar(
                bmT_b[:], pb[5][:, 0:128], 1.0, None, op0=OP.min), inc="dve")
            BMT_DVE = kb.cnt["dve"]
            kb.wait("tensor", "dve", BMT_DVE)

            waf = [sb(f"waf{i}", [128, DM], F32, cx) for i in range(2)]
            w1f = [sb(f"w1f{i}", [128, 128], F32, cx) for i in range(2)]
            wmb = [sb(f"wmb{i}", [128, DM], BF16, cx) for i in range(2)]
            wsem = [kb.sem(f"wa{i}") for i in range(2)]
            w1_evac = {0: 0, 1: 0}
            wm_done = {0: 0, 1: 0}
            qst_wm = {0: 0, 1: 0}
            ind_sb = indT_sb
            indB_sb = sb("indB_sb", [128, DM], BF16, cx)
            kb.dma("gpsimd", indB_sb[:], blk_ind, "qw")
            kb.wait("tensor", "qw", kb.cnt["qw"])
            for c in range(KC if lvl >= 1 else 0):
                sl = c % 2
                kb.wait("sync", "dve", wm_done[sl])
                kb.dma("sync", waf[sl][:], waT[c * 128:(c + 1) * 128, :], wsem[sl])
                kb.wait("tensor", "act", w1_evac[sl])
                kb.emit("tensor", lambda e, c=c, sl=sl: e.matmul(
                    pb[4 + sl][:, 0:128], indB_sb[:, c * 128:(c + 1) * 128], bmT_b[:],
                    start=True, stop=True), inc="pe")
                kb.wait("scalar", "pe", kb.cnt["pe"])
                kb.wait("scalar", "dve", wm_done[sl])
                kb.emit("scalar", lambda s, sl=sl: s.copy(
                    w1f[sl][:], pb[4 + sl][:, 0:128]), inc="act")
                w1_evac[sl] = kb.cnt["act"]
                kb.wait("vector", "act", w1_evac[sl])
                kb.wait("vector", wsem[sl], kb.cnt[wsem[sl]])
                kb.wait("vector", "qst", qst_wm[sl])
                kb.emit("vector", lambda v, sl=sl: v.tensor_tensor(
                    wmb[sl][:].rearrange("p (i w) -> p i w", w=32),
                    waf[sl][:].rearrange("p (i w) -> p i w", w=32),
                    w1f[sl][:].unsqueeze(2).to_broadcast([128, 128, 32]),
                    op=OP.mult), inc="dve")
                wm_done[sl] = kb.cnt["dve"]
                kb.wait("sync", "dve", wm_done[sl])
                qst_wm[sl] = kb.dma("sync", wm_blk[c].rearrange("n p f -> p n f"),
                                    wmb[sl][:].rearrange("p (n f) -> p n f", n=8), "qst")
            kb.wait("sync", "qst", kb.cnt["qst"])
        kb.barrier()

        if lvl >= 2:
            ln_phase("l1", x0, dst_dram=h1T_sc)

        dstA = y if stage == 'A' else x1_sc
        with ExitStack() as cx:
            GA = 2 if lvl >= 3 else 0
            hgrp = sb("hgrp", [128, KC * 1024], BF16, cx)
            wmt = [sb(f"wmt{i}", [128, KC * 512], BF16, cx) for i in range(2)]
            bias_bc = sb("bias_bc", [128, DM], F32, cx)
            brow = sb("brow", [1, DM], F32, cx)
            x0t = [sb(f"x0t{i}", [128, 512], F32, cx) for i in range(4)]
            x1t = [sb(f"x1t{i}", [128, 512], F32, cx) for i in range(4)]
            hsem = kb.sem("hg")
            wmsem = [kb.sem(f"wms{i}") for i in range(2)]
            xsem = [kb.sem(f"x0s{i}") for i in range(4)]

            if GA:
                kb.dma("sync", bias_bc[:], bias_a.partition_broadcast(128).squeeze(1),
                       "qx")
                kb.wait("vector", "qx", kb.cnt["qx"])
                kb.dma("sync", hgrp[:].rearrange("k (c f) -> k c f", c=KC),
                       h1T_sc[:, :, 0:1024].rearrange("c k f -> k c f"), hsem)
                kb.dma("sync", wmt[0][:].rearrange("p (c f) -> p c f", c=KC),
                       wm_blk[:, 0].rearrange("c p f -> p c f"), wmsem[0])
                kb.dma("sync", wmt[1][:].rearrange("p (c f) -> p c f", c=KC),
                       wm_blk[:, 1].rearrange("c p f -> p c f"), wmsem[1])

            ev_done = {i: 0 for i in range(4)}
            x_used = {i: 0 for i in range(4)}
            x1_st = {i: 0 for i in range(4)}
            pe_after_chunk = 0
            xi = 0
            for g in range(GA):
                if g == 1:
                    kb.wait("sync", "pe", kb.cnt["pe"])
                    kb.dma("sync", hgrp[:].rearrange("k (c f) -> k c f", c=KC),
                           h1T_sc[:, :, 1024:2048].rearrange("c k f -> k c f"), hsem)
                kb.wait("tensor", hsem, kb.cnt[hsem])
                for ncb in range(8):
                    wsl = ncb % 2
                    kb.wait("tensor", wmsem[wsl], kb.cnt[wmsem[wsl]])
                    for t in range(8):
                        tt = g * 8 + t
                        bank = t % 4
                        xs = xi % 4
                        kb.wait("sync", "dve", x_used[xs])
                        kb.dma("sync", x0t[xs][:],
                               x0[tt * 128:(tt + 1) * 128, ncb * 512:(ncb + 1) * 512],
                               xsem[xs])
                        kb.wait("tensor", "dve", ev_done[bank])
                        for kc in range(KC):
                            kb.emit("tensor", lambda e, hb_=hgrp, kc=kc, t=t,
                                    wsl=wsl, bank=bank: e.matmul(
                                pb[bank][:],
                                hb_[:, kc * 1024 + t * 128: kc * 1024 + (t + 1) * 128],
                                wmt[wsl][:, kc * 512:(kc + 1) * 512],
                                start=(kc == 0), stop=(kc == KC - 1)),
                                inc=("pe" if kc == KC - 1 else None))
                        mm_fin = kb.cnt["pe"]
                        kb.wait("vector", "pe", mm_fin)
                        kb.wait("vector", xsem[xs], kb.cnt[xsem[xs]])
                        kb.wait("vector", "qst", x1_st[xs])
                        kb.emit("vector", lambda v, bank=bank, ncb=ncb, xs=xs:
                                v.scalar_tensor_tensor(
                                    x1t[xs][:], pb[bank][:], 1.0,
                                    bias_bc[:, ncb * 512:(ncb + 1) * 512],
                                    op0=OP.mult, op1=OP.add))
                        kb.emit("vector", lambda v, xs=xs: v.tensor_tensor(
                            x1t[xs][:], x1t[xs][:], x0t[xs][:], op=OP.add), inc="dve")
                        ev_done[bank] = kb.cnt["dve"]
                        x_used[xs] = kb.cnt["dve"]
                        kb.wait("sync", "dve", ev_done[bank])
                        x1_st[xs] = kb.dma(
                            "sync", dstA[tt * 128:(tt + 1) * 128,
                                         ncb * 512:(ncb + 1) * 512], x1t[xs][:], "qst")
                        xi += 1
                    nxt = ncb + 2
                    if g == 0 and nxt >= 8:
                        pass
                    if nxt < 8:
                        kb.wait("sync", "pe", kb.cnt["pe"])
                        kb.dma("sync", wmt[nxt % 2][:].rearrange("p (c f) -> p c f", c=KC),
                               wm_blk[:, nxt].rearrange("c p f -> p c f"),
                               wmsem[nxt % 2])
                if g == 0:
                    kb.wait("sync", "pe", kb.cnt["pe"])
                    kb.dma("sync", wmt[0][:].rearrange("p (c f) -> p c f", c=KC),
                           wm_blk[:, 0].rearrange("c p f -> p c f"), wmsem[0])
                    kb.dma("sync", wmt[1][:].rearrange("p (c f) -> p c f", c=KC),
                           wm_blk[:, 1].rearrange("c p f -> p c f"), wmsem[1])
            kb.wait("sync", "qst", kb.cnt["qst"])
        kb.barrier()

        def ffn_phase(tag, src_res, hT_dram, dst):
            with ExitStack() as cx:
                hgrp = sb(f"{tag}hg", [128, KC * 1024], BF16, cx)
                atb = sb(f"{tag}at", [128, KC * 512], BF16, cx)
                btb = sb(f"{tag}bt", [128, 4 * DM], BF16, cx)
                wqb = sb(f"{tag}wq", [128, KC * 16], BF16, cx)
                keys_sb = sb(f"{tag}ke", [16, NE], BF16, cx)
                repE_sb = sb(f"{tag}re", [NE, BOT], BF16, cx)
                z_bf = sb(f"{tag}z", [128, 4 * 1024], BF16, cx)
                zg = sb(f"{tag}zg", [128, 4 * 1024], BF16, cx)
                qr_bf = sb(f"{tag}qr", [16, 1024], BF16, cx)
                scT_f = sb(f"{tag}sT", [NE, 1024], F32, cx)
                sc_f = sb(f"{tag}sc", [128, 8 * NE], F32, cx)
                sc8a = sb(f"{tag}m8a", [128, 8], F32, cx)
                sc8b = sb(f"{tag}m8b", [128, 8], F32, cx)
                work = sb(f"{tag}wk", [128, NE], F32, cx)
                e64 = sb(f"{tag}e64", [128, NE], F32, cx)
                eg = sb(f"{tag}eg", [128, NE], F32, cx)
                negm1 = sb(f"{tag}nm1", [128, 1], F32, cx)
                gsum = sb(f"{tag}gs", [128, 1], F32, cx)
                rg = sb(f"{tag}rg", [128, 1], F32, cx)
                gate_f = sb(f"{tag}gf", [128, NE], F32, cx)
                gT_bf = sb(f"{tag}gT", [NE, 1024], BF16, cx)
                xres = [sb(f"{tag}xr{i}", [128, 512], F32, cx) for i in range(4)]
                xout = [sb(f"{tag}xo{i}", [128, 512], F32, cx) for i in range(4)]
                hsem = kb.sem(f"{tag}h")
                wsem = kb.sem(f"{tag}w")
                rsem = [kb.sem(f"{tag}r{i}") for i in range(4)]

                kb.dma("gpsimd", atb[:].rearrange("p (c n) -> p c n", c=KC),
                       AT.rearrange("(c p) n -> p c n", p=128), wsem)
                kb.dma("gpsimd", btb[:].rearrange("p (b o) -> p b o", b=4),
                       BT.rearrange("(b p) o -> p b o", p=128), wsem)
                kb.dma("gpsimd", wqb[:].rearrange("p (c r) -> p c r", c=KC),
                       WqT.rearrange("(c p) r -> p c r", p=128), wsem)
                kb.dma("gpsimd", keys_sb[:], keysT, wsem)
                kb.dma("gpsimd", repE_sb[:], repE, wsem)
                WQ = kb.cnt[wsem]
                kb.wait("tensor", wsem, WQ)

                bank_g = {i: 0 for i in range(6)}
                bank_sem = {i: "dve" for i in range(6)}
                pe_half_end = 0
                xi = 0
                x_used = {i: 0 for i in range(4)}
                xo_st = {i: 0 for i in range(4)}
                for th in range(2):
                    if th == 1:
                        kb.wait("sync", "pe", kb.cnt["pe"])
                    kb.dma("sync", hgrp[:].rearrange("k (c f) -> k c f", c=KC),
                           hT_dram[:, :, th * 1024:(th + 1) * 1024]
                           .rearrange("c k f -> k c f"), hsem)
                    kb.wait("tensor", hsem, kb.cnt[hsem])
                    for tc in range(2):
                        kb.wait("tensor", bank_sem[4], bank_g[4])
                        for kc in range(KC):
                            kb.emit("tensor", lambda e, tc=tc, kc=kc: e.matmul(
                                pb[4][0:16, :],
                                wqb[:, kc * 16:(kc + 1) * 16],
                                hgrp[:, kc * 1024 + tc * 512: kc * 1024 + (tc + 1) * 512],
                                start=(kc == 0), stop=(kc == KC - 1)),
                                inc=("pe" if kc == KC - 1 else None))
                        kb.wait("scalar", "pe", kb.cnt["pe"])
                        kb.emit("scalar", lambda s, tc=tc: s.copy(
                            qr_bf[0:16, tc * 512:(tc + 1) * 512], pb[4][0:16, :]),
                            inc="act")
                        bank_g[4], bank_sem[4] = kb.cnt["act"], "act"
                        kb.wait("tensor", "act", bank_g[4])
                        kb.emit("tensor", lambda e, tc=tc: e.matmul(
                            pb[4][0:64, :], keys_sb[:],
                            qr_bf[0:16, tc * 512:(tc + 1) * 512],
                            start=True, stop=True), inc="pe")
                        kb.wait("scalar", "pe", kb.cnt["pe"])
                        kb.emit("scalar", lambda s, tc=tc: s.copy(
                            scT_f[0:64, tc * 512:(tc + 1) * 512], pb[4][0:64, :]),
                            inc="act")
                        bank_g[4], bank_sem[4] = kb.cnt["act"], "act"
                    kb.wait("tensor", "act", kb.cnt["act"])
                    kb.wait("tensor", bank_sem[5], bank_g[5])
                    for t in range(8):
                        kb.emit("tensor", lambda e, t=t: e.transpose(
                            pb[5][:, t * 64:(t + 1) * 64],
                            scT_f[0:64, t * 128:(t + 1) * 128],
                            idf_sb[0:64, 0:64]), inc=("pe" if t == 7 else None))
                    kb.wait("vector", "pe", kb.cnt["pe"])
                    kb.emit("vector", lambda v: v.tensor_copy(sc_f[:], pb[5][:]),
                            inc="dve")
                    bank_g[5], bank_sem[5] = kb.cnt["dve"], "dve"

                    def gate_tile(t):
                        sc_sl = sc_f[:, t * 64:(t + 1) * 64]
                        kb.wait("vector", "dve", kb.cnt["dve"])
                        kb.emit("vector", lambda v, sc_sl=sc_sl: v.max(sc8a[:], sc_sl),
                                inc="dve")
                        kb.wait("vector", "dve", kb.cnt["dve"])
                        kb.emit("vector", lambda v, sc_sl=sc_sl: v.match_replace(
                            work[:], sc8a[:], sc_sl, NEGBIG), inc="dve")
                        kb.wait("vector", "dve", kb.cnt["dve"])
                        kb.emit("vector", lambda v: v.max(sc8b[:], work[:]), inc="dve")
                        kb.wait("vector", "dve", kb.cnt["dve"])
                        kb.emit("vector", lambda v: v.tensor_scalar(
                            negm1[:], sc8a[:, 0:1], -1.0, None, op0=OP.mult),
                            inc="dve")
                        kb.wait("scalar", "dve", kb.cnt["dve"])
                        kb.emit("scalar", lambda s, sc_sl=sc_sl: s.activation(
                            e64[:], sc_sl, AF.Exp, bias=negm1[:], scale=1.0),
                            inc="act")
                        kb.wait("vector", "act", kb.cnt["act"])
                        kb.emit("vector", lambda v, sc_sl=sc_sl: v.scalar_tensor_tensor(
                            eg[:], sc_sl, sc8b[:, 7:8], e64[:],
                            op0=OP.is_ge, op1=OP.mult, accum_out=gsum[:]), inc="dve")
                        kb.wait("vector", "dve", kb.cnt["dve"])
                        kb.emit("vector", lambda v: v.reciprocal(rg[:], gsum[:]),
                                inc="dve")
                        kb.wait("vector", "dve", kb.cnt["dve"])
                        kb.emit("vector", lambda v: v.tensor_scalar(
                            gate_f[:], eg[:], rg[:], None, op0=OP.mult), inc="dve")
                        kb.wait("tensor", "dve", kb.cnt["dve"])
                        if t % 4 == 0:
                            kb.wait("tensor", bank_sem[5], bank_g[5])
                        kb.emit("tensor", lambda e, t=t: e.transpose(
                            pb[5][0:64, (t % 4) * 128:((t % 4) + 1) * 128],
                            gate_f[:], idf_sb[:]), inc="pe")
                        if t % 4 == 3:
                            kb.wait("scalar", "pe", kb.cnt["pe"])
                            kb.emit("scalar", lambda s, t=t: s.copy(
                                gT_bf[0:64, (t - 3) * 128:(t + 1) * 128],
                                pb[5][0:64, :]), inc="act")
                            bank_g[5], bank_sem[5] = kb.cnt["act"], "act"

                    for blk in range(8):
                        nb, tc = blk // 2, blk % 2
                        bank = blk % 4
                        kb.wait("tensor", bank_sem[bank], bank_g[bank])
                        for kc in range(KC):
                            kb.emit("tensor", lambda e, nb=nb, tc=tc, kc=kc,
                                    bank=bank: e.matmul(
                                pb[bank][:],
                                atb[:, kc * 512 + nb * 128: kc * 512 + (nb + 1) * 128],
                                hgrp[:, kc * 1024 + tc * 512: kc * 1024 + (tc + 1) * 512],
                                start=(kc == 0), stop=(kc == KC - 1)),
                                inc=("pe" if kc == KC - 1 else None))
                        kb.wait("scalar", "pe", kb.cnt["pe"])
                        kb.emit("scalar", lambda s, nb=nb, tc=tc, bank=bank: s.copy(
                            z_bf[:, nb * 1024 + tc * 512: nb * 1024 + (tc + 1) * 512],
                            pb[bank][:]), inc="act")
                        bank_g[bank], bank_sem[bank] = kb.cnt["act"], "act"
                        gate_tile(blk)

                    GT_ACT = kb.cnt["act"]
                    for nb in range(4):
                        for tc in range(2):
                            bank = 2 + (nb * 2 + tc) % 2
                            kb.wait("tensor", bank_sem[bank], bank_g[bank])
                            kb.wait("tensor", "act", GT_ACT)
                            kb.emit("tensor", lambda e, nb=nb, tc=tc, bank=bank: e.matmul(
                                pb[bank][:],
                                repE_sb[0:64, nb * 128:(nb + 1) * 128],
                                gT_bf[0:64, tc * 512:(tc + 1) * 512],
                                start=True, stop=True), inc="pe")
                            kb.wait("vector", "pe", kb.cnt["pe"])
                            kb.wait("vector", "act", kb.cnt["act"])
                            kb.emit("vector", lambda v, nb=nb, tc=tc, bank=bank:
                                    v.tensor_tensor(
                                zg[:, nb * 1024 + tc * 512: nb * 1024 + (tc + 1) * 512],
                                z_bf[:, nb * 1024 + tc * 512: nb * 1024 + (tc + 1) * 512],
                                pb[bank][:], op=OP.mult), inc="dve")
                            bank_g[bank], bank_sem[bank] = kb.cnt["dve"], "dve"
                    ZG_DVE = kb.cnt["dve"]
                    kb.wait("tensor", "dve", ZG_DVE)
                    for t in range(8):
                        tt = th * 8 + t
                        for oh in range(2):
                            for oc in range(4):
                                ocg = oh * 4 + oc
                                xs = (xi + oc) % 4
                                kb.wait("sync", "dve", x_used[xs])
                                kb.dma("sync", xres[xs][:],
                                       src_res[tt * 128:(tt + 1) * 128,
                                               ocg * 512:(ocg + 1) * 512], rsem[xs])
                                kb.wait("tensor", bank_sem[oc], bank_g[oc])
                            for nb in range(4):
                                for oc in range(4):
                                    ocg = oh * 4 + oc
                                    kb.emit("tensor", lambda e, nb=nb, t=t, ocg=ocg,
                                            oc=oc: e.matmul(
                                        pb[oc][:],
                                        zg[:, nb * 1024 + t * 128: nb * 1024 + (t + 1) * 128],
                                        btb[:, nb * 4096 + ocg * 512: nb * 4096 + (ocg + 1) * 512],
                                        start=(nb == 0), stop=(nb == 3)),
                                        inc=("pe" if (nb == 3 and oc == 3) else None))
                            kb.wait("vector", "pe", kb.cnt["pe"])
                            for oc in range(4):
                                ocg = oh * 4 + oc
                                xs = (xi + oc) % 4
                                kb.wait("vector", rsem[xs], kb.cnt[rsem[xs]])
                                kb.wait("vector", "qst", xo_st[xs])
                                kb.emit("vector", lambda v, oc=oc, xs=xs:
                                        v.scalar_tensor_tensor(
                                            xout[xs][:], pb[oc][:], FFN_SCALE,
                                            xres[xs][:], op0=OP.mult, op1=OP.add),
                                        inc="dve")
                                bank_g[oc], bank_sem[oc] = kb.cnt["dve"], "dve"
                                x_used[xs] = kb.cnt["dve"]
                                kb.wait("sync", "dve", kb.cnt["dve"])
                                xo_st[xs] = kb.dma(
                                    "sync", dst[tt * 128:(tt + 1) * 128,
                                                ocg * 512:(ocg + 1) * 512],
                                    xout[xs][:], "qst")
                            xi += 4
                kb.wait("sync", "qst", kb.cnt["qst"])
            kb.barrier()

        if lvl >= 4:
            ln_phase("l2", x1_sc, dst_dram=hT_sc)
            ffn_phase("f1", x1_sc, hT_sc, y if stage == 'B' else x2_sc)

        if lvl >= 5:
            with ExitStack() as cx:
                h3T = sb("h3T", [128, KC * S], BF16, cx)
                ln_phase("l3", x2_sc, dst_sb=h3T[:].rearrange(
                    "p (c f) -> p c f", c=KC))

                wslab = [sb(f"ws{i}", [128, KC * 128], BF16, cx) for i in range(2)]
                q_sb = [sb(f"q_sb{i}", [128, S], BF16, cx) for i in range(2)]
                k_sb = [sb(f"k_sb{i}", [128, S], BF16, cx) for i in range(2)]
                v_sb = [sb(f"v_sb{i}", [128, S], BF16, cx) for i in range(2)]
                vt_bf = sb("vt_bf", [128, S], BF16, cx)
                s_sb = sb("s_sb", [128, S], F32, cx)
                p_bf = [sb(f"p_bf{i}", [128, S], BF16, cx) for i in range(2)]
                pt_sb = sb("pt_sb", [128, S], BF16, cx)
                alibi_sb = sb("alibi_sb", [128, S], F32, cx)
                atn_h = [sb(f"atn_h{i}", [128, 512], BF16, cx) for i in range(2)]
                bq_c = [sb(f"bq_c{i}", [128, 1], F32, cx) for i in range(2)]
                bk_c = [sb(f"bk_c{i}", [128, 1], F32, cx) for i in range(2)]
                bv_c = [sb(f"bv_c{i}", [128, 1], F32, cx) for i in range(2)]
                bqs_c = [sb(f"bqs_c{i}", [128, 1], F32, cx) for i in range(2)]
                mrow = [sb(f"mrow{i}", [128, 1], F32, cx) for i in range(2)]
                negm = [sb(f"negm{i}", [128, 1], F32, cx) for i in range(2)]
                ssum = [sb(f"ssum{i}", [128, 1], F32, cx) for i in range(2)]
                rr = [sb(f"rr{i}", [128, 1], F32, cx) for i in range(2)]
                ssem = [kb.sem(f"wsl{i}") for i in range(2)]
                bsem = [kb.sem(f"bqkv{i}") for i in range(2)]
                alsem = kb.sem("albi")

                bank_g = {i: 0 for i in range(6)}
                bank_sem = {i: "dve" for i in range(6)}
                pt_g = {0: 0, 1: 0}
                atn_st = {0: 0, 1: 0}
                albi_used_v = [0]
                slab_pe = {0: 0, 1: 0}
                bias_used = {0: 0, 1: 0}
                vt_act = {0: 0, 1: 0}
                qkv_pe_done = {0: 0, 1: 0}
                pnorm_d = {}
                exp_a = {}

                def slab_load(i, sl):
                    h_, part = i // 3, i % 3
                    kb.wait("gpsimd", "pe", slab_pe[sl])
                    kb.dma("gpsimd",
                           wslab[sl][:].rearrange("p (c o) -> p c o", c=KC),
                           wqkvT[:, part * DM + h_ * 128: part * DM + (h_ + 1) * 128]
                           .rearrange("(c p) o -> p c o", p=128), ssem[sl])

                def emit_bias_alibi(h):
                    p = h % 2
                    kb.wait("sync", "act", bias_used[p])
                    for j, lo in ((0, h * 128), (1, DM + h * 128), (2, 2 * DM + h * 128)):
                        dstc = (bq_c, bk_c, bv_c)[j]
                        kb.dma("sync", dstc[p][:],
                               bqkv[0:1, lo:lo + 128].rearrange("a k -> k a"), bsem[p])
                    kb.wait("vector", bsem[p], kb.cnt[bsem[p]])
                    kb.emit("vector", lambda v, p=p: v.tensor_scalar(
                        bqs_c[p][:], bq_c[p][:], Q_SCALE, None, op0=OP.mult), inc="dve")
                    kb.wait("scalar", "dve", kb.cnt["dve"])

                def emit_alibi(h):
                    kb.wait("sync", "dve", albi_used_v[0])
                    kb.dma("sync", alibi_sb[:],
                           alibi_c[h:h + 1, :].partition_broadcast(128).squeeze(1),
                           alsem)

                def emit_qkv_unit(h, unit):
                    p = h % 2
                    i = h * 3
                    if unit < 3:
                        part = unit
                        sl = (i + part) % 2
                        kb.wait("tensor", ssem[sl], kb.cnt[ssem[sl]])
                        for bank in range(4):
                            kb.wait("tensor", bank_sem[bank], bank_g[bank])
                        for kc in range(KC):
                            for tc in range(4):
                                kb.emit("tensor", lambda e, kc=kc, tc=tc, sl=sl:
                                        e.matmul(
                                    pb[tc][:],
                                    wslab[sl][:, kc * 128:(kc + 1) * 128],
                                    h3T[:, kc * S + tc * 512: kc * S + (tc + 1) * 512],
                                    start=(kc == 0), stop=(kc == KC - 1)),
                                    inc=("pe" if (kc == KC - 1 and tc == 3) else None))
                        slab_pe[sl] = kb.cnt["pe"]
                        kb.wait("scalar", "pe", kb.cnt["pe"])
                        for tc in range(4):
                            if part == 0:
                                kb.emit("scalar", lambda s, tc=tc, p=p: s.activation(
                                    q_sb[p][:, tc * 512:(tc + 1) * 512], pb[tc][:],
                                    AF.Identity, bias=bqs_c[p][:], scale=Q_SCALE),
                                    inc="act")
                            elif part == 1:
                                kb.emit("scalar", lambda s, tc=tc, p=p: s.activation(
                                    k_sb[p][:, tc * 512:(tc + 1) * 512], pb[tc][:],
                                    AF.Identity, bias=bk_c[p][:], scale=1.0), inc="act")
                            else:
                                kb.emit("scalar", lambda s, tc=tc, p=p: s.activation(
                                    vt_bf[:, tc * 512:(tc + 1) * 512], pb[tc][:],
                                    AF.Identity, bias=bv_c[p][:], scale=1.0), inc="act")
                            bank_g[tc], bank_sem[tc] = kb.cnt["act"], "act"
                        bias_used[p] = kb.cnt["act"]
                        if part == 2:
                            vt_act[p] = kb.cnt["act"]
                        if i + part + 2 < 3 * H:
                            slab_load(i + part + 2, sl)
                    else:
                        g4 = unit - 3
                        bk = g4 % 2
                        kb.wait("tensor", "act", vt_act[p])
                        kb.wait("tensor", "act", pt_g[bk])
                        for j in range(4):
                            kb_i = g4 * 4 + j
                            kb.emit("tensor", lambda e, kb_i=kb_i, j=j, bk=bk:
                                    e.transpose(
                                pTb[bk][:, j * 128:(j + 1) * 128],
                                vt_bf[:, kb_i * 128:(kb_i + 1) * 128], id_sb[:]),
                                inc=("pe" if j == 3 else None))
                        kb.wait("scalar", "pe", kb.cnt["pe"])
                        kb.emit("scalar", lambda s, g4=g4, bk=bk, p=p: s.copy(
                            v_sb[p][:, g4 * 512:(g4 + 1) * 512],
                            pTb[bk][:, 0:512]), inc="act")
                        pt_g[bk] = kb.cnt["act"]

                def emit_S1(h, qi):
                    p, sp = h % 2, qi % 2
                    L = (qi + 1) * 128
                    nch = (L + 511) // 512
                    kb.wait("vector", "act", exp_a.get((h * NT + qi) - 1, 0))
                    for ch in range(nch):
                        w = min(512, L - ch * 512)
                        bank = 4 + ch % 2
                        kb.wait("tensor", bank_sem[bank], bank_g[bank])
                        kb.emit("tensor", lambda e, qi=qi, ch=ch, w=w, bank=bank, p=p:
                                e.matmul(
                            pb[bank][:, 0:w],
                            q_sb[p][:, qi * 128:(qi + 1) * 128],
                            k_sb[p][:, ch * 512: ch * 512 + w],
                            start=True, stop=True), inc="pe")
                        kb.wait("vector", "pe", kb.cnt["pe"])
                        kb.wait("vector", alsem, kb.cnt[alsem])
                        kb.emit("vector", lambda v, ch=ch, w=w, bank=bank:
                                v.tensor_tensor(
                            s_sb[:, ch * 512: ch * 512 + w], pb[bank][:, 0:w],
                            alibi_sb[:, ch * 512: ch * 512 + w], op=OP.add),
                            inc="dve")
                        bank_g[bank], bank_sem[bank] = kb.cnt["dve"], "dve"
                        albi_used_v[0] = kb.cnt["dve"]
                    kb.emit("vector", lambda v, L=L: v.tensor_tensor(
                        s_sb[:, L - 128:L], s_sb[:, L - 128:L], causal_sb[:],
                        op=OP.add), inc="dve")
                    kb.wait("vector", "dve", kb.cnt["dve"])
                    kb.emit("vector", lambda v, L=L, sp=sp: v.tensor_reduce(
                        mrow[sp][:], s_sb[:, 0:L], axis=AX.X, op=OP.max),
                        inc="dve")
                    kb.wait("vector", "dve", kb.cnt["dve"])
                    kb.emit("vector", lambda v, sp=sp: v.tensor_scalar(
                        negm[sp][:], mrow[sp][:], -1.0, None, op0=OP.mult), inc="dve")
                    kb.wait("scalar", "dve", kb.cnt["dve"])
                    kb.emit("scalar", lambda s, L=L, sp=sp: s.activation(
                        p_bf[sp][:, 0:L], s_sb[:, 0:L], AF.Exp, bias=negm[sp][:],
                        scale=1.0, accum_out=ssum[sp][:]), inc="act")
                    exp_a[h * NT + qi] = kb.cnt["act"]

                def emit_S1fin(h, qi):
                    sp = qi % 2
                    L = (qi + 1) * 128
                    kb.wait("vector", "act", exp_a[h * NT + qi])
                    kb.emit("vector", lambda v, sp=sp: v.reciprocal(
                        rr[sp][:], ssum[sp][:]), inc="dve")
                    kb.wait("vector", "dve", kb.cnt["dve"])
                    kb.emit("vector", lambda v, L=L, sp=sp: v.tensor_scalar(
                        p_bf[sp][:, 0:L], p_bf[sp][:, 0:L], rr[sp][:], None,
                        op0=OP.mult), inc="dve")
                    pnorm_d[qi] = kb.cnt["dve"]

                def emit_S2(h, qi):
                    p, sp = h % 2, qi % 2
                    kb.wait("tensor", "dve", pnorm_d[qi])
                    for kb_i in range(qi + 1):
                        bk = (kb_i // 4) % 2
                        if kb_i % 4 == 0:
                            kb.wait("tensor", "act", pt_g[bk])
                        last = (kb_i % 4 == 3) or (kb_i == qi)
                        kb.emit("tensor", lambda e, kb_i=kb_i, bk=bk, sp=sp:
                                e.transpose(
                            pTb[bk][:, (kb_i % 4) * 128:((kb_i % 4) + 1) * 128],
                            p_bf[sp][:, kb_i * 128:(kb_i + 1) * 128], id_sb[:]),
                            inc=("pe" if last else None))
                        if last:
                            lo = (kb_i // 4) * 512
                            n = (kb_i % 4 + 1) * 128
                            kb.wait("scalar", "pe", kb.cnt["pe"])
                            kb.emit("scalar", lambda s, lo=lo, n=n, bk=bk: s.copy(
                                pt_sb[:, lo:lo + n], pTb[bk][:, 0:n]), inc="act")
                            pt_g[bk] = kb.cnt["act"]
                    PT_ACT = kb.cnt["act"]
                    av = 4 + qi % 2
                    kb.wait("tensor", bank_sem[av], bank_g[av])
                    kb.wait("tensor", "act", PT_ACT)
                    for kb_i in range(qi + 1):
                        kb.emit("tensor", lambda e, kb_i=kb_i, av=av, qi=qi, p=p:
                                e.matmul(
                            pb[av][:, 0:128],
                            v_sb[p][:, kb_i * 128:(kb_i + 1) * 128],
                            pt_sb[:, kb_i * 128:(kb_i + 1) * 128],
                            start=(kb_i == 0), stop=(kb_i == qi)),
                            inc=("pe" if kb_i == qi else None))
                    slot = (qi // 4) % 2
                    kb.wait("scalar", "pe", kb.cnt["pe"])
                    kb.wait("scalar", "qst", atn_st[slot])
                    kb.emit("scalar", lambda s, qi=qi, av=av, slot=slot: s.copy(
                        atn_h[slot][:, (qi % 4) * 128:((qi % 4) + 1) * 128],
                        pb[av][:, 0:128]), inc="act")
                    bank_g[av], bank_sem[av] = kb.cnt["act"], "act"
                    if qi % 4 == 3:
                        kb.wait("sync", "act", kb.cnt["act"])
                        atn_st[slot] = kb.dma(
                            "sync", attnT_sc[h][:, (qi - 3) * 128:(qi + 1) * 128],
                            atn_h[slot][:], "qst")

                slab_load(0, 0)
                slab_load(1, 1)
                emit_bias_alibi(0)
                emit_alibi(0)
                for u in range(7):
                    emit_qkv_unit(0, u)
                for h in range(H):
                    if h + 1 < H:
                        emit_bias_alibi(h + 1)
                    for qi in range(NT):
                        emit_S1(h, qi)
                        if qi == NT - 1 and h + 1 < H:
                            emit_alibi(h + 1)
                        if qi >= 1:
                            emit_S1fin(h, qi - 1)
                            emit_S2(h, qi - 1)
                        if h + 1 < H and qi < 7:
                            emit_qkv_unit(h + 1, qi)
                    emit_S1fin(h, NT - 1)
                    emit_S2(h, NT - 1)
                kb.wait("sync", "qst", kb.cnt["qst"])
            kb.barrier()


            dstC = y if stage == 'C' else x3_sc
            with ExitStack() as cx:
                atn_all = sb("atn_all", [128, KC * S], BF16, cx)
                woslab = [sb(f"wo{i}", [128, KC * 512], BF16, cx) for i in range(2)]
                bout_bc = [sb(f"bout{i}", [128, 512], F32, cx) for i in range(2)]
                xres = [sb(f"c3xr{i}", [128, 512], F32, cx) for i in range(2)]
                xout = [sb(f"c3xo{i}", [128, 512], F32, cx) for i in range(2)]
                asem = kb.sem("c3a")
                wsem2 = [kb.sem(f"c3w{i}") for i in range(2)]
                rsem2 = [kb.sem(f"c3r{i}") for i in range(2)]

                bsem3 = [kb.sem(f"c3b{i}") for i in range(2)]
                bb_used = {0: 0, 1: 0}
                kb.dma("sync", bout_bc[0][:],
                       bout[0:1, 0:512].partition_broadcast(128).squeeze(1), bsem3[0])
                kb.dma("sync", bout_bc[1][:],
                       bout[0:1, 512:1024].partition_broadcast(128).squeeze(1),
                       bsem3[1])
                kb.dma("sync", atn_all[:].rearrange("p (c f) -> p c f", c=KC),
                       attnT_sc.rearrange("c k f -> k c f"), asem)

                def wo_load(fc, sl):
                    kb.dma("gpsimd",
                           woslab[sl][:].rearrange("p (c o) -> p c o", c=KC),
                           woutT[:, fc * 512:(fc + 1) * 512]
                           .rearrange("(c p) o -> p c o", p=128), wsem2[sl])

                wo_load(0, 0)
                wo_load(1, 1)
                kb.wait("tensor", asem, kb.cnt[asem])
                bank_g = {i: 0 for i in range(4)}
                slab_pe2 = {0: 0, 1: 0}
                x_used = {i: 0 for i in range(2)}
                xo_st = {i: 0 for i in range(2)}
                xi = 0
                for fc in range(8):
                    sl = fc % 2
                    kb.wait("tensor", wsem2[sl], kb.cnt[wsem2[sl]])
                    for t in range(NT):
                        bank = t % 4
                        xs = xi % 2
                        kb.wait("sync", "dve", x_used[xs])
                        kb.dma("sync", xres[xs][:],
                               x2_sc[t * 128:(t + 1) * 128, fc * 512:(fc + 1) * 512],
                               rsem2[xs])
                        kb.wait("tensor", "dve", bank_g[bank])
                        for kc in range(KC):
                            kb.emit("tensor", lambda e, kc=kc, t=t, sl=sl, bank=bank:
                                    e.matmul(
                                pb[bank][:],
                                atn_all[:, kc * S + t * 128: kc * S + (t + 1) * 128],
                                woslab[sl][:, kc * 512:(kc + 1) * 512],
                                start=(kc == 0), stop=(kc == KC - 1)),
                                inc=("pe" if kc == KC - 1 else None))
                        slab_pe2[sl] = kb.cnt["pe"]
                        kb.wait("vector", "pe", kb.cnt["pe"])
                        kb.wait("vector", rsem2[xs], kb.cnt[rsem2[xs]])
                        kb.wait("vector", "qst", xo_st[xs])
                        if t == 0:
                            kb.wait("vector", bsem3[sl], kb.cnt[bsem3[sl]])
                        kb.emit("vector", lambda v, bank=bank, sl=sl, xs=xs:
                                v.scalar_tensor_tensor(
                                    xout[xs][:], pb[bank][:], 1.0,
                                    bout_bc[sl][:],
                                    op0=OP.mult, op1=OP.add))
                        kb.emit("vector", lambda v, xs=xs: v.tensor_tensor(
                            xout[xs][:], xout[xs][:], xres[xs][:], op=OP.add),
                            inc="dve")
                        bank_g[bank] = kb.cnt["dve"]
                        x_used[xs] = kb.cnt["dve"]
                        bb_used[sl] = kb.cnt["dve"]
                        kb.wait("sync", "dve", kb.cnt["dve"])
                        xo_st[xs] = kb.dma(
                            "sync", dstC[t * 128:(t + 1) * 128,
                                         fc * 512:(fc + 1) * 512], xout[xs][:], "qst")
                        xi += 1
                    if fc + 2 < 8:
                        kb.wait("sync", "dve", bb_used[sl])
                        kb.dma("sync", bout_bc[sl][:],
                               bout[0:1, (fc + 2) * 512:(fc + 3) * 512]
                               .partition_broadcast(128).squeeze(1), bsem3[sl])
                        kb.wait("gpsimd", "pe", slab_pe2[sl])
                        wo_load(fc + 2, sl)
                kb.wait("sync", "qst", kb.cnt["qst"])
            kb.barrier()

        if lvl >= 6:
            ln_phase("l4", x3_sc, dst_dram=hT_sc)
            ffn_phase("f2", x3_sc, hT_sc, y)

        kb.wait("sync", "qst", kb.cnt["qst"])
        kb.wait("gpsimd", "qw", kb.cnt["qw"])
        kb.barrier()

        block = ctx.enter_context(nc.Block())
        for eng in KB.ENGS:
            def make(eng):
                def body(e):
                    for fn in kb.q[eng]:
                        fn(e)
                return body
            getattr(block, eng)(make(eng))

    return nc


def _prep_inputs(inputs):
    f32 = np.float32
    bf16 = ml_dtypes.bfloat16
    x = np.ascontiguousarray(inputs["L0_attn_a_data"], dtype=f32)
    mask = np.ascontiguousarray(inputs["L0_attn_a_mask"], dtype=np.int32)
    shared = {
        "mask_i": mask,
        "waT": np.ascontiguousarray(np.asarray(inputs["L0_attn_a_weight"], dtype=f32).T),
        "bias_a": np.asarray(inputs["L0_attn_a_bias"], dtype=f32).reshape(1, DM),
        "wqkvT": np.ascontiguousarray(np.asarray(inputs["L0_attn_b_w_qkv"], dtype=f32).T),
        "bqkv": np.asarray(inputs["L0_attn_b_b_qkv"], dtype=f32).reshape(1, 3 * DM),
        "woutT": np.ascontiguousarray(np.asarray(inputs["L0_attn_b_w_out"], dtype=f32).T),
        "bout": np.asarray(inputs["L0_attn_b_b_out"], dtype=f32).reshape(1, DM),
        "AT": np.ascontiguousarray(np.asarray(inputs["L0_ffn_A_w"], dtype=f32).T),
        "BT": np.ascontiguousarray(np.asarray(inputs["L0_ffn_B_w"], dtype=f32).T),
        "WqT": np.ascontiguousarray(np.asarray(inputs["L0_ffn_Wq_w"], dtype=f32).T),
        "keysT": np.ascontiguousarray(np.asarray(inputs["L0_ffn_keys"], dtype=f32).T),
        "id_bf": np.eye(128, dtype=f32).astype(bf16),
        "id_f": np.eye(128, dtype=f32),
        "causal_c": np.where(np.arange(128)[:, None] >= np.arange(128)[None, :],
                             0.0, NEGBIG).astype(f32),
        "pos_c": np.ascontiguousarray(
            np.broadcast_to((np.arange(S) - (S - 1)).astype(f32), (128, S))),
        "alibi_c": np.ascontiguousarray(
            np.array(SLOPES, dtype=f32)[:, None] *
            (np.arange(S) - (S - 1)).astype(f32)[None, :]),
        "blk_ind": (np.arange(DM)[None, :] // 32 ==
                    np.arange(128)[:, None]).astype(bf16),
        "blk_indT": np.ascontiguousarray(
            (np.arange(DM)[:, None] // 32 == np.arange(128)[None, :]).astype(bf16)),
        "repE": (np.arange(BOT)[None, :] // RANK ==
                 np.arange(NE)[:, None]).astype(bf16),
    }
    in_maps = []
    for c in range(8):
        m = dict(shared)
        m["x0"] = np.ascontiguousarray(x[c])
        in_maps.append(m)
    return in_maps


def _run(inputs, stage_=None, debug=()):
    st = stage_ or STAGE
    key = (st, tuple(sorted(debug)))
    if key not in _CACHE:
        _CACHE[key] = build_program(st, frozenset(debug))
    nc = _CACHE[key]
    in_maps = _prep_inputs(inputs)
    return run_bass_kernel_spmd(nc, in_maps, core_ids=list(range(8)))


def kernel(**inputs):
    res = _run(inputs)
    return np.stack([r["y"] for r in res.results], axis=0).astype(np.float32)


# version 04
# speedup vs baseline: 1.1717x; 1.1717x over previous
import math
from contextlib import ExitStack

import numpy as np
import ml_dtypes
import concourse.bass as bass
import concourse.mybir as mybir
from concourse.alu_op_type import AluOpType as OP
from concourse.bass_utils import run_bass_kernel_spmd

F32 = mybir.dt.float32
BF16 = mybir.dt.bfloat16
I32 = mybir.dt.int32
AF = mybir.ActivationFunctionType
AX = mybir.AxisListType

S, DM, H, HD = 2048, 4096, 32, 128
NT = S // 128
KC = DM // 128
NE, RANK, TOPK, RDIM = 64, 8, 16, 16
BOT = NE * RANK
FFN_SCALE = 2.0
NEGBIG = -1.0e9
EPS = 1e-5
SLOPES = [2.0 ** (-(i + 1) * 8.0 / 32.0) for i in range(H)]
Q_SCALE = 1.0 / math.sqrt(HD)

_CACHE = {}
STAGE = 'full'


class KB:
    ENGS = ("tensor", "vector", "scalar", "gpsimd", "sync")

    def __init__(self, nc, ctx):
        self.nc = nc
        self.ctx = ctx
        self.q = {e: [] for e in self.ENGS}
        self.sems = {}
        self.cnt = {}

    def sem(self, name):
        if name not in self.sems:
            self.sems[name] = self.ctx.enter_context(self.nc.semaphore(name))
            self.cnt[name] = 0
        return name

    def emit(self, eng, fn, inc=None, n=1):
        cell = []

        def runner(e, fn=fn, cell=cell):
            inst = fn(e)
            for (sh, k) in cell:
                inst.then_inc(sh, k)
            return inst
        self.q[eng].append(runner)
        self.last_cell = getattr(self, "last_cell", {})
        self.last_cell[eng] = cell
        if inc is None:
            return None
        cell.append((self.sems[inc], n))
        self.cnt[inc] += n
        return self.cnt[inc]

    def mark(self, eng, sem_name, n=1):
        self.last_cell[eng].append((self.sems[sem_name], n))
        self.cnt[sem_name] += n
        return self.cnt[sem_name]

    def wait(self, eng, sem_name, val):
        if val <= 0:
            return
        s = self.sems[sem_name]
        self.q[eng].append(lambda e, s=s, v=val: e.wait_ge(s, v))

    def barrier(self, engines=None):
        engines = engines or self.ENGS
        self.sem("bar")
        tgt = self.cnt["bar"] + len(engines)
        for e in engines:
            if e == "vector":
                self.emit(e, lambda v: v.memset(self.dum_v[:], 0.0), inc="bar")
            elif e == "scalar":
                self.emit(e, lambda x: x.copy(self.dum_s[:], self.dum_v[:]), inc="bar")
            elif e == "gpsimd":
                self.emit(e, lambda g: g.memset(self.dum_g[:], 0.0), inc="bar")
            else:
                self.emit(e, lambda x: x.nop(), inc="bar")
        for e in engines:
            self.wait(e, "bar", tgt)

    def dma(self, qeng, dst, src, sem):
        return self.emit(qeng, lambda e, d=dst, s=src: e.dma_start(d, s), inc=sem, n=16)


def build_program(stage, debug):
    lvl = {'0a': 0, '0b': 1, '0c': 2, 'A': 3, 'B': 4, 'C': 5, 'full': 6}[stage]
    nc = bass.Bass("TRN2", target_bir_lowering=False)

    def dram(name, shape, dtype, kind):
        if kind == "Internal" and name in debug:
            kind = "ExternalOutput"
        return nc.dram_tensor(name, shape, dtype, kind=kind).ap()

    x0 = dram("x0", [S, DM], F32, "ExternalInput")
    mask_i = dram("mask_i", [DM, DM], I32, "ExternalInput")
    waT = dram("waT", [DM, DM], F32, "ExternalInput")
    bias_a = dram("bias_a", [1, DM], F32, "ExternalInput")
    wqkvT = dram("wqkvT", [DM, 3 * DM], F32, "ExternalInput")
    bqkv = dram("bqkv", [1, 3 * DM], F32, "ExternalInput")
    woutT = dram("woutT", [DM, DM], F32, "ExternalInput")
    bout = dram("bout", [1, DM], F32, "ExternalInput")
    AT = dram("AT", [DM, BOT], F32, "ExternalInput")
    BT = dram("BT", [BOT, DM], F32, "ExternalInput")
    WqT = dram("WqT", [DM, RDIM], F32, "ExternalInput")
    keysT = dram("keysT", [RDIM, NE], F32, "ExternalInput")
    id_bf = dram("id_bf", [128, 128], BF16, "ExternalInput")
    id_f = dram("id_f", [128, 128], F32, "ExternalInput")
    causal_c = dram("causal_c", [128, 128], F32, "ExternalInput")
    pos_c = dram("pos_c", [128, S], F32, "ExternalInput")
    alibi_c = dram("alibi_c", [H, S], F32, "ExternalInput")
    blk_ind = dram("blk_ind", [128, DM], BF16, "ExternalInput")
    blk_indT = dram("blk_indT", [DM, 128], BF16, "ExternalInput")
    repE = dram("repE", [NE, BOT], BF16, "ExternalInput")

    y = dram("y", [S, DM], F32, "ExternalOutput")
    wm_blk = dram("wm_blk", [KC, 8, 128, 512], BF16, "Internal")
    h1T_sc = dram("h1T_sc", [KC, 128, S], BF16, "Internal")
    hT_sc = dram("hT_sc", [KC, 128, S], BF16, "Internal")
    x1_sc = dram("x1_sc", [S, DM], F32, "Internal")
    x2_sc = dram("x2_sc", [S, DM], F32, "Internal")
    x3_sc = dram("x3_sc", [S, DM], F32, "Internal")
    attnT_sc = dram("attnT_sc", [KC, 128, S], BF16, "Internal")

    with ExitStack() as ctx:
        kb = KB(nc, ctx)
        for s_ in ("pe", "dve", "act", "gp", "qx", "qw", "qst", "bar"):
            kb.sem(s_)

        def sb(name, shape, dtype, cx=None):
            return (cx or ctx).enter_context(nc.sbuf_tensor(name, shape, dtype))

        kb.dum_v = sb("dum_v", [128, 1], F32)
        kb.dum_s = sb("dum_s", [128, 1], F32)
        kb.dum_g = sb("dum_g", [128, 1], F32)
        id_sb = sb("id_sb", [128, 128], BF16)
        idf_sb = sb("idf_sb", [128, 128], F32)
        causal_sb = sb("causal_sb", [128, 128], F32)

        pb = [ctx.enter_context(nc.psum_tensor(f"pb{i}", [128, 512], F32)) for i in range(6)]
        pTb = [ctx.enter_context(nc.psum_tensor(f"pT{i}", [128, 1024], BF16)) for i in range(2)]

        kb.dma("sync", id_sb[:], id_bf, "qx")
        kb.dma("sync", idf_sb[:], id_f, "qx")
        kb.dma("sync", causal_sb[:], causal_c, "qx")
        CONST_QX = kb.cnt["qx"]

        def ln_phase(tag, src_dram, dst_dram=None, dst_sb=None):
            with ExitStack() as cx:
                xf = [sb(f"{tag}xf{i}", [128, DM], F32, cx) for i in range(2)]
                hb = sb(f"{tag}hb", [128, DM], BF16, cx)
                stage_b = None
                if dst_dram is not None:
                    stage_b = [sb(f"{tag}st{i}", [128, KC * 128], BF16, cx) for i in range(2)]
                bn48 = sb(f"{tag}bn", [128, 48], F32, cx)
                mv = sb(f"{tag}mv", [128, 2], F32, cx)
                sd = sb(f"{tag}sd", [128, 1], F32, cx)
                rstd = sb(f"{tag}rs", [128, 1], F32, cx)
                negmr = sb(f"{tag}nm", [128, 1], F32, cx)
                xsem = [kb.sem(f"{tag}x{i}") for i in range(2)]

                act_norm = {}
                evac_done = {0: 0, 1: 0}
                pe_done = {}
                st_done = {0: 0, 1: 0}
                for t in range(NT):
                    sl = t % 2
                    if t >= 2:
                        kb.wait("sync", "act", act_norm[t - 2])
                    kb.dma("sync", xf[sl][:], src_dram[t * 128:(t + 1) * 128, :], xsem[sl])
                    if t >= 1:
                        kb.wait("vector", "act", act_norm[t - 1])
                    kb.wait("vector", xsem[sl], kb.cnt[xsem[sl]])
                    for c in range(8):
                        kb.emit("vector", lambda v, sl=sl, c=c: v.bn_stats(
                            bn48[:, c * 6:(c + 1) * 6],
                            xf[sl][:, c * 512:(c + 1) * 512]), inc="dve")
                    kb.wait("vector", "dve", kb.cnt["dve"])
                    kb.emit("vector", lambda v: v.bn_aggr(mv[:], bn48[:]), inc="dve")
                    kb.wait("vector", "dve", kb.cnt["dve"])
                    kb.emit("vector", lambda v: v.tensor_scalar(
                        mv[:, 1:2], mv[:, 1:2], EPS, None, op0=OP.add), inc="dve")
                    kb.wait("scalar", "dve", kb.cnt["dve"])
                    kb.emit("scalar", lambda s: s.activation(
                        sd[:], mv[:, 1:2], AF.Sqrt), inc="act")
                    kb.wait("vector", "act", kb.cnt["act"])
                    kb.emit("vector", lambda v: v.reciprocal(rstd[:], sd[:]), inc="dve")
                    kb.wait("vector", "dve", kb.cnt["dve"])
                    kb.emit("vector", lambda v: v.scalar_tensor_tensor(
                        negmr[:], mv[:, 0:1], -1.0, rstd[:],
                        op0=OP.mult, op1=OP.mult), inc="dve")
                    kb.wait("scalar", "dve", kb.cnt["dve"])
                    if t >= 1:
                        kb.wait("scalar", "pe", pe_done[t - 1])
                    kb.emit("scalar", lambda s, sl=sl: s.activation(
                        hb[:], xf[sl][:], AF.Identity,
                        bias=negmr[:], scale=rstd[:]), inc="act")
                    act_norm[t] = kb.cnt["act"]
                    kb.wait("tensor", "act", act_norm[t])
                    for grp in range(8):
                        bk = grp % 2
                        kb.wait("tensor", "act", evac_done[bk])
                        for j in range(4):
                            kc = grp * 4 + j
                            kb.emit("tensor", lambda e, kc=kc, j=j, bk=bk: e.transpose(
                                pTb[bk][:, j * 128:(j + 1) * 128],
                                hb[:, kc * 128:(kc + 1) * 128], id_sb[:]),
                                inc=("pe" if j == 3 else None))
                        kb.wait("scalar", "pe", kb.cnt["pe"])
                        if dst_sb is not None:
                            kb.emit("scalar", lambda s, grp=grp, bk=bk, t=t: s.copy(
                                dst_sb[:, grp * 4:(grp + 1) * 4, t * 128:(t + 1) * 128],
                                pTb[bk][:, 0:512]
                                .rearrange("p (c f) -> p c f", f=128)), inc="act")
                        else:
                            if grp == 0:
                                kb.wait("scalar", "qst", st_done[sl])
                            kb.emit("scalar", lambda s, grp=grp, bk=bk, sl=sl: s.copy(
                                stage_b[sl][:, grp * 512:(grp + 1) * 512],
                                pTb[bk][:, 0:512]), inc="act")
                        evac_done[bk] = kb.cnt["act"]
                    pe_done[t] = kb.cnt["pe"]
                    if dst_dram is not None:
                        kb.wait("sync", "act", kb.cnt["act"])
                        kb.dma("sync",
                               dst_dram[:, :, t * 128:(t + 1) * 128]
                               .rearrange("c k f -> k c f"),
                               stage_b[sl][:].rearrange("k (c f) -> k c f", c=KC), "qst")
                        st_done[sl] = kb.cnt["qst"]
                kb.wait("sync", "qst", kb.cnt["qst"])
            kb.barrier()

        bmT_b = sb("bmT_b", [128, 128], BF16)
        with ExitStack() as cx:
            mkf = [sb(f"mkf{i}", [128, DM], F32, cx) for i in range(2)]
            t1 = [sb(f"t1_{i}", [128, 128], BF16, cx) for i in range(2)]
            t1f = sb("t1f", [128, 128], F32, cx)
            indT_sb = sb("indT_sb", [128, KC * 128], BF16, cx)
            msem = [kb.sem(f"mk{i}") for i in range(2)]

            kb.dma("gpsimd", indT_sb[:].rearrange("p (c i) -> p c i", c=KC),
                   blk_indT.rearrange("(c p) i -> p c i", p=128), "qw")
            IND_QW = kb.cnt["qw"]
            kb.wait("tensor", "qw", IND_QW)
            t1_done = {0: 0, 1: 0}
            mm_done = {0: 0, 1: 0}
            for c in range(KC):
                sl = c % 2
                kb.wait("gpsimd", "dve", t1_done[sl])
                kb.dma("gpsimd", mkf[sl][:], mask_i[c * 128:(c + 1) * 128, :], msem[sl])
                kb.wait("vector", msem[sl], kb.cnt[msem[sl]])
                kb.wait("vector", "pe", mm_done[sl])
                kb.emit("vector", lambda v, sl=sl: v.tensor_reduce(
                    t1f[:], mkf[sl][:].rearrange("p (j w) -> p j w", w=32),
                    axis=AX.X, op=OP.add), inc="dve")
                kb.wait("vector", "dve", kb.cnt["dve"])
                kb.emit("vector", lambda v, sl=sl: v.tensor_copy(
                    t1[sl][:], t1f[:]), inc="dve")
                t1_done[sl] = kb.cnt["dve"]
                kb.wait("tensor", "dve", t1_done[sl])
                mm_done[sl] = kb.emit("tensor", lambda e, sl=sl, c=c: e.matmul(
                    pb[5][:, 0:128], t1[sl][:], indT_sb[:, c * 128:(c + 1) * 128],
                    start=(c == 0), stop=(c == KC - 1)), inc="pe")
            kb.wait("vector", "pe", kb.cnt["pe"])
            kb.emit("vector", lambda v: v.tensor_scalar(
                bmT_b[:], pb[5][:, 0:128], 1.0, None, op0=OP.min), inc="dve")
            BMT_DVE = kb.cnt["dve"]
            kb.wait("tensor", "dve", BMT_DVE)

            waf = [sb(f"waf{i}", [128, DM], F32, cx) for i in range(2)]
            w1f = [sb(f"w1f{i}", [128, 128], F32, cx) for i in range(2)]
            wmb = [sb(f"wmb{i}", [128, DM], BF16, cx) for i in range(2)]
            wsem = [kb.sem(f"wa{i}") for i in range(2)]
            w1_evac = {0: 0, 1: 0}
            wm_done = {0: 0, 1: 0}
            qst_wm = {0: 0, 1: 0}
            ind_sb = indT_sb
            indB_sb = sb("indB_sb", [128, DM], BF16, cx)
            kb.dma("gpsimd", indB_sb[:], blk_ind, "qw")
            kb.wait("tensor", "qw", kb.cnt["qw"])
            for c in range(KC if lvl >= 1 else 0):
                sl = c % 2
                kb.wait("sync", "dve", wm_done[sl])
                kb.dma("sync", waf[sl][:], waT[c * 128:(c + 1) * 128, :], wsem[sl])
                kb.wait("tensor", "act", w1_evac[sl])
                kb.emit("tensor", lambda e, c=c, sl=sl: e.matmul(
                    pb[4 + sl][:, 0:128], indB_sb[:, c * 128:(c + 1) * 128], bmT_b[:],
                    start=True, stop=True), inc="pe")
                kb.wait("scalar", "pe", kb.cnt["pe"])
                kb.wait("scalar", "dve", wm_done[sl])
                kb.emit("scalar", lambda s, sl=sl: s.copy(
                    w1f[sl][:], pb[4 + sl][:, 0:128]), inc="act")
                w1_evac[sl] = kb.cnt["act"]
                kb.wait("vector", "act", w1_evac[sl])
                kb.wait("vector", wsem[sl], kb.cnt[wsem[sl]])
                kb.wait("vector", "qst", qst_wm[sl])
                kb.emit("vector", lambda v, sl=sl: v.tensor_tensor(
                    wmb[sl][:].rearrange("p (i w) -> p i w", w=32),
                    waf[sl][:].rearrange("p (i w) -> p i w", w=32),
                    w1f[sl][:].unsqueeze(2).to_broadcast([128, 128, 32]),
                    op=OP.mult), inc="dve")
                wm_done[sl] = kb.cnt["dve"]
                kb.wait("sync", "dve", wm_done[sl])
                qst_wm[sl] = kb.dma("sync", wm_blk[c].rearrange("n p f -> p n f"),
                                    wmb[sl][:].rearrange("p (n f) -> p n f", n=8), "qst")
            kb.wait("sync", "qst", kb.cnt["qst"])
        kb.barrier()

        if lvl >= 2:
            ln_phase("l1", x0, dst_dram=h1T_sc)

        dstA = y if stage == 'A' else x1_sc
        with ExitStack() as cx:
            GA = 2 if lvl >= 3 else 0
            hgrp = sb("hgrp", [128, KC * 1024], BF16, cx)
            wmt = [sb(f"wmt{i}", [128, KC * 512], BF16, cx) for i in range(2)]
            bias_bc = sb("bias_bc", [128, DM], F32, cx)
            brow = sb("brow", [1, DM], F32, cx)
            x0t = [sb(f"x0t{i}", [128, 512], F32, cx) for i in range(4)]
            x1t = [sb(f"x1t{i}", [128, 512], F32, cx) for i in range(4)]
            hsem = kb.sem("hg")
            wmsem = [kb.sem(f"wms{i}") for i in range(2)]
            xsem = [kb.sem(f"x0s{i}") for i in range(4)]

            if GA:
                kb.dma("sync", bias_bc[:], bias_a.partition_broadcast(128).squeeze(1),
                       "qx")
                kb.wait("vector", "qx", kb.cnt["qx"])
                kb.dma("sync", hgrp[:].rearrange("k (c f) -> k c f", c=KC),
                       h1T_sc[:, :, 0:1024].rearrange("c k f -> k c f"), hsem)
                kb.dma("sync", wmt[0][:].rearrange("p (c f) -> p c f", c=KC),
                       wm_blk[:, 0].rearrange("c p f -> p c f"), wmsem[0])
                kb.dma("sync", wmt[1][:].rearrange("p (c f) -> p c f", c=KC),
                       wm_blk[:, 1].rearrange("c p f -> p c f"), wmsem[1])

            ev_done = {i: 0 for i in range(4)}
            x_used = {i: 0 for i in range(4)}
            x1_st = {i: 0 for i in range(4)}
            pe_after_chunk = 0
            xi = 0
            for g in range(GA):
                if g == 1:
                    kb.wait("sync", "pe", kb.cnt["pe"])
                    kb.dma("sync", hgrp[:].rearrange("k (c f) -> k c f", c=KC),
                           h1T_sc[:, :, 1024:2048].rearrange("c k f -> k c f"), hsem)
                kb.wait("tensor", hsem, kb.cnt[hsem])
                for ncb in range(8):
                    wsl = ncb % 2
                    kb.wait("tensor", wmsem[wsl], kb.cnt[wmsem[wsl]])
                    for t in range(8):
                        tt = g * 8 + t
                        bank = t % 4
                        xs = xi % 4
                        kb.wait("sync", "dve", x_used[xs])
                        kb.dma("sync", x0t[xs][:],
                               x0[tt * 128:(tt + 1) * 128, ncb * 512:(ncb + 1) * 512],
                               xsem[xs])
                        kb.wait("tensor", "dve", ev_done[bank])
                        for kc in range(KC):
                            kb.emit("tensor", lambda e, hb_=hgrp, kc=kc, t=t,
                                    wsl=wsl, bank=bank: e.matmul(
                                pb[bank][:],
                                hb_[:, kc * 1024 + t * 128: kc * 1024 + (t + 1) * 128],
                                wmt[wsl][:, kc * 512:(kc + 1) * 512],
                                start=(kc == 0), stop=(kc == KC - 1)),
                                inc=("pe" if kc == KC - 1 else None))
                        mm_fin = kb.cnt["pe"]
                        kb.wait("vector", "pe", mm_fin)
                        kb.wait("vector", xsem[xs], kb.cnt[xsem[xs]])
                        kb.wait("vector", "qst", x1_st[xs])
                        kb.emit("vector", lambda v, bank=bank, ncb=ncb, xs=xs:
                                v.scalar_tensor_tensor(
                                    x1t[xs][:], pb[bank][:], 1.0,
                                    bias_bc[:, ncb * 512:(ncb + 1) * 512],
                                    op0=OP.mult, op1=OP.add))
                        kb.emit("vector", lambda v, xs=xs: v.tensor_tensor(
                            x1t[xs][:], x1t[xs][:], x0t[xs][:], op=OP.add), inc="dve")
                        ev_done[bank] = kb.cnt["dve"]
                        x_used[xs] = kb.cnt["dve"]
                        kb.wait("sync", "dve", ev_done[bank])
                        x1_st[xs] = kb.dma(
                            "sync", dstA[tt * 128:(tt + 1) * 128,
                                         ncb * 512:(ncb + 1) * 512], x1t[xs][:], "qst")
                        xi += 1
                    nxt = ncb + 2
                    if g == 0 and nxt >= 8:
                        pass
                    if nxt < 8:
                        kb.wait("sync", "pe", kb.cnt["pe"])
                        kb.dma("sync", wmt[nxt % 2][:].rearrange("p (c f) -> p c f", c=KC),
                               wm_blk[:, nxt].rearrange("c p f -> p c f"),
                               wmsem[nxt % 2])
                if g == 0:
                    kb.wait("sync", "pe", kb.cnt["pe"])
                    kb.dma("sync", wmt[0][:].rearrange("p (c f) -> p c f", c=KC),
                           wm_blk[:, 0].rearrange("c p f -> p c f"), wmsem[0])
                    kb.dma("sync", wmt[1][:].rearrange("p (c f) -> p c f", c=KC),
                           wm_blk[:, 1].rearrange("c p f -> p c f"), wmsem[1])
            kb.wait("sync", "qst", kb.cnt["qst"])
        kb.barrier()

        def ffn_phase(tag, src_res, hT_dram, dst):
            with ExitStack() as cx:
                hgrp = sb(f"{tag}hg", [128, KC * 1024], BF16, cx)
                atb = sb(f"{tag}at", [128, KC * 512], BF16, cx)
                btb = sb(f"{tag}bt", [128, 4 * DM], BF16, cx)
                wqb = sb(f"{tag}wq", [128, KC * 16], BF16, cx)
                keys_sb = sb(f"{tag}ke", [16, NE], BF16, cx)
                repE_sb = sb(f"{tag}re", [NE, BOT], BF16, cx)
                z_bf = sb(f"{tag}z", [128, 4 * 1024], BF16, cx)
                zg = sb(f"{tag}zg", [128, 4 * 1024], BF16, cx)
                qr_bf = sb(f"{tag}qr", [16, 1024], BF16, cx)
                scT_f = sb(f"{tag}sT", [NE, 1024], F32, cx)
                sc_f = sb(f"{tag}sc", [128, 8 * NE], F32, cx)
                sc8a = sb(f"{tag}m8a", [128, 8], F32, cx)
                sc8b = sb(f"{tag}m8b", [128, 8], F32, cx)
                work = sb(f"{tag}wk", [128, NE], F32, cx)
                e64 = sb(f"{tag}e64", [128, NE], F32, cx)
                eg = sb(f"{tag}eg", [128, NE], F32, cx)
                negm1 = sb(f"{tag}nm1", [128, 1], F32, cx)
                gsum = sb(f"{tag}gs", [128, 1], F32, cx)
                rg = sb(f"{tag}rg", [128, 1], F32, cx)
                gate_f = sb(f"{tag}gf", [128, NE], F32, cx)
                gT_bf = sb(f"{tag}gT", [NE, 1024], BF16, cx)
                xres = [sb(f"{tag}xr{i}", [128, 512], F32, cx) for i in range(4)]
                xout = [sb(f"{tag}xo{i}", [128, 512], F32, cx) for i in range(4)]
                hsem = kb.sem(f"{tag}h")
                wsem = kb.sem(f"{tag}w")
                rsem = [kb.sem(f"{tag}r{i}") for i in range(4)]

                kb.dma("gpsimd", atb[:].rearrange("p (c n) -> p c n", c=KC),
                       AT.rearrange("(c p) n -> p c n", p=128), wsem)
                kb.dma("gpsimd", btb[:].rearrange("p (b o) -> p b o", b=4),
                       BT.rearrange("(b p) o -> p b o", p=128), wsem)
                kb.dma("gpsimd", wqb[:].rearrange("p (c r) -> p c r", c=KC),
                       WqT.rearrange("(c p) r -> p c r", p=128), wsem)
                kb.dma("gpsimd", keys_sb[:], keysT, wsem)
                kb.dma("gpsimd", repE_sb[:], repE, wsem)
                WQ = kb.cnt[wsem]
                kb.wait("tensor", wsem, WQ)

                bank_g = {i: 0 for i in range(6)}
                bank_sem = {i: "dve" for i in range(6)}
                pe_half_end = 0
                xi = 0
                x_used = {i: 0 for i in range(4)}
                xo_st = {i: 0 for i in range(4)}
                for th in range(2):
                    if th == 1:
                        kb.wait("sync", "pe", kb.cnt["pe"])
                    kb.dma("sync", hgrp[:].rearrange("k (c f) -> k c f", c=KC),
                           hT_dram[:, :, th * 1024:(th + 1) * 1024]
                           .rearrange("c k f -> k c f"), hsem)
                    kb.wait("tensor", hsem, kb.cnt[hsem])
                    for tc in range(2):
                        kb.wait("tensor", bank_sem[4], bank_g[4])
                        for kc in range(KC):
                            kb.emit("tensor", lambda e, tc=tc, kc=kc: e.matmul(
                                pb[4][0:16, :],
                                wqb[:, kc * 16:(kc + 1) * 16],
                                hgrp[:, kc * 1024 + tc * 512: kc * 1024 + (tc + 1) * 512],
                                start=(kc == 0), stop=(kc == KC - 1)),
                                inc=("pe" if kc == KC - 1 else None))
                        kb.wait("scalar", "pe", kb.cnt["pe"])
                        kb.emit("scalar", lambda s, tc=tc: s.copy(
                            qr_bf[0:16, tc * 512:(tc + 1) * 512], pb[4][0:16, :]),
                            inc="act")
                        bank_g[4], bank_sem[4] = kb.cnt["act"], "act"
                        kb.wait("tensor", "act", bank_g[4])
                        kb.emit("tensor", lambda e, tc=tc: e.matmul(
                            pb[4][0:64, :], keys_sb[:],
                            qr_bf[0:16, tc * 512:(tc + 1) * 512],
                            start=True, stop=True), inc="pe")
                        kb.wait("scalar", "pe", kb.cnt["pe"])
                        kb.emit("scalar", lambda s, tc=tc: s.copy(
                            scT_f[0:64, tc * 512:(tc + 1) * 512], pb[4][0:64, :]),
                            inc="act")
                        bank_g[4], bank_sem[4] = kb.cnt["act"], "act"
                    kb.wait("tensor", "act", kb.cnt["act"])
                    kb.wait("tensor", bank_sem[5], bank_g[5])
                    for t in range(8):
                        kb.emit("tensor", lambda e, t=t: e.transpose(
                            pb[5][:, t * 64:(t + 1) * 64],
                            scT_f[0:64, t * 128:(t + 1) * 128],
                            idf_sb[0:64, 0:64]), inc=("pe" if t == 7 else None))
                    kb.wait("vector", "pe", kb.cnt["pe"])
                    kb.emit("vector", lambda v: v.tensor_copy(sc_f[:], pb[5][:]),
                            inc="dve")
                    bank_g[5], bank_sem[5] = kb.cnt["dve"], "dve"

                    def gate_tile(t):
                        sc_sl = sc_f[:, t * 64:(t + 1) * 64]
                        kb.wait("vector", "dve", kb.cnt["dve"])
                        kb.emit("vector", lambda v, sc_sl=sc_sl: v.max(sc8a[:], sc_sl),
                                inc="dve")
                        kb.wait("vector", "dve", kb.cnt["dve"])
                        kb.emit("vector", lambda v, sc_sl=sc_sl: v.match_replace(
                            work[:], sc8a[:], sc_sl, NEGBIG), inc="dve")
                        kb.wait("vector", "dve", kb.cnt["dve"])
                        kb.emit("vector", lambda v: v.max(sc8b[:], work[:]), inc="dve")
                        kb.wait("vector", "dve", kb.cnt["dve"])
                        kb.emit("vector", lambda v: v.tensor_scalar(
                            negm1[:], sc8a[:, 0:1], -1.0, None, op0=OP.mult),
                            inc="dve")
                        kb.wait("scalar", "dve", kb.cnt["dve"])
                        kb.emit("scalar", lambda s, sc_sl=sc_sl: s.activation(
                            e64[:], sc_sl, AF.Exp, bias=negm1[:], scale=1.0),
                            inc="act")
                        kb.wait("vector", "act", kb.cnt["act"])
                        kb.emit("vector", lambda v, sc_sl=sc_sl: v.scalar_tensor_tensor(
                            eg[:], sc_sl, sc8b[:, 7:8], e64[:],
                            op0=OP.is_ge, op1=OP.mult, accum_out=gsum[:]), inc="dve")
                        kb.wait("vector", "dve", kb.cnt["dve"])
                        kb.emit("vector", lambda v: v.reciprocal(rg[:], gsum[:]),
                                inc="dve")
                        kb.wait("vector", "dve", kb.cnt["dve"])
                        kb.emit("vector", lambda v: v.tensor_scalar(
                            gate_f[:], eg[:], rg[:], None, op0=OP.mult), inc="dve")
                        kb.wait("tensor", "dve", kb.cnt["dve"])
                        if t % 4 == 0:
                            kb.wait("tensor", bank_sem[5], bank_g[5])
                        kb.emit("tensor", lambda e, t=t: e.transpose(
                            pb[5][0:64, (t % 4) * 128:((t % 4) + 1) * 128],
                            gate_f[:], idf_sb[:]), inc="pe")
                        if t % 4 == 3:
                            kb.wait("scalar", "pe", kb.cnt["pe"])
                            kb.emit("scalar", lambda s, t=t: s.copy(
                                gT_bf[0:64, (t - 3) * 128:(t + 1) * 128],
                                pb[5][0:64, :]), inc="act")
                            bank_g[5], bank_sem[5] = kb.cnt["act"], "act"

                    for blk in range(8):
                        nb, tc = blk // 2, blk % 2
                        bank = blk % 4
                        kb.wait("tensor", bank_sem[bank], bank_g[bank])
                        for kc in range(KC):
                            kb.emit("tensor", lambda e, nb=nb, tc=tc, kc=kc,
                                    bank=bank: e.matmul(
                                pb[bank][:],
                                atb[:, kc * 512 + nb * 128: kc * 512 + (nb + 1) * 128],
                                hgrp[:, kc * 1024 + tc * 512: kc * 1024 + (tc + 1) * 512],
                                start=(kc == 0), stop=(kc == KC - 1)),
                                inc=("pe" if kc == KC - 1 else None))
                        kb.wait("scalar", "pe", kb.cnt["pe"])
                        kb.emit("scalar", lambda s, nb=nb, tc=tc, bank=bank: s.copy(
                            z_bf[:, nb * 1024 + tc * 512: nb * 1024 + (tc + 1) * 512],
                            pb[bank][:]), inc="act")
                        bank_g[bank], bank_sem[bank] = kb.cnt["act"], "act"
                        gate_tile(blk)

                    GT_ACT = kb.cnt["act"]
                    for nb in range(4):
                        for tc in range(2):
                            bank = 2 + (nb * 2 + tc) % 2
                            kb.wait("tensor", bank_sem[bank], bank_g[bank])
                            kb.wait("tensor", "act", GT_ACT)
                            kb.emit("tensor", lambda e, nb=nb, tc=tc, bank=bank: e.matmul(
                                pb[bank][:],
                                repE_sb[0:64, nb * 128:(nb + 1) * 128],
                                gT_bf[0:64, tc * 512:(tc + 1) * 512],
                                start=True, stop=True), inc="pe")
                            kb.wait("vector", "pe", kb.cnt["pe"])
                            kb.wait("vector", "act", kb.cnt["act"])
                            kb.emit("vector", lambda v, nb=nb, tc=tc, bank=bank:
                                    v.tensor_tensor(
                                zg[:, nb * 1024 + tc * 512: nb * 1024 + (tc + 1) * 512],
                                z_bf[:, nb * 1024 + tc * 512: nb * 1024 + (tc + 1) * 512],
                                pb[bank][:], op=OP.mult), inc="dve")
                            bank_g[bank], bank_sem[bank] = kb.cnt["dve"], "dve"
                    ZG_DVE = kb.cnt["dve"]
                    kb.wait("tensor", "dve", ZG_DVE)
                    for t in range(8):
                        tt = th * 8 + t
                        for oh in range(2):
                            for oc in range(4):
                                ocg = oh * 4 + oc
                                xs = (xi + oc) % 4
                                kb.wait("sync", "dve", x_used[xs])
                                kb.dma("sync", xres[xs][:],
                                       src_res[tt * 128:(tt + 1) * 128,
                                               ocg * 512:(ocg + 1) * 512], rsem[xs])
                                kb.wait("tensor", bank_sem[oc], bank_g[oc])
                            for nb in range(4):
                                for oc in range(4):
                                    ocg = oh * 4 + oc
                                    kb.emit("tensor", lambda e, nb=nb, t=t, ocg=ocg,
                                            oc=oc: e.matmul(
                                        pb[oc][:],
                                        zg[:, nb * 1024 + t * 128: nb * 1024 + (t + 1) * 128],
                                        btb[:, nb * 4096 + ocg * 512: nb * 4096 + (ocg + 1) * 512],
                                        start=(nb == 0), stop=(nb == 3)),
                                        inc=("pe" if (nb == 3 and oc == 3) else None))
                            kb.wait("vector", "pe", kb.cnt["pe"])
                            for oc in range(4):
                                ocg = oh * 4 + oc
                                xs = (xi + oc) % 4
                                kb.wait("vector", rsem[xs], kb.cnt[rsem[xs]])
                                kb.wait("vector", "qst", xo_st[xs])
                                kb.emit("vector", lambda v, oc=oc, xs=xs:
                                        v.scalar_tensor_tensor(
                                            xout[xs][:], pb[oc][:], FFN_SCALE,
                                            xres[xs][:], op0=OP.mult, op1=OP.add),
                                        inc="dve")
                                bank_g[oc], bank_sem[oc] = kb.cnt["dve"], "dve"
                                x_used[xs] = kb.cnt["dve"]
                                kb.wait("sync", "dve", kb.cnt["dve"])
                                xo_st[xs] = kb.dma(
                                    "sync", dst[tt * 128:(tt + 1) * 128,
                                                ocg * 512:(ocg + 1) * 512],
                                    xout[xs][:], "qst")
                            xi += 4
                kb.wait("sync", "qst", kb.cnt["qst"])
            kb.barrier()

        if lvl >= 4:
            ln_phase("l2", x1_sc, dst_dram=hT_sc)
            ffn_phase("f1", x1_sc, hT_sc, y if stage == 'B' else x2_sc)

        if lvl >= 5:
            with ExitStack() as cx:
                h3T = sb("h3T", [128, KC * S], BF16, cx)
                ln_phase("l3", x2_sc, dst_sb=h3T[:].rearrange(
                    "p (c f) -> p c f", c=KC))

                wslab = [sb(f"ws{i}", [128, KC * 128], BF16, cx) for i in range(2)]
                q_sb = [sb(f"q_sb{i}", [128, S], BF16, cx) for i in range(2)]
                k_sb = [sb(f"k_sb{i}", [128, S], BF16, cx) for i in range(2)]
                v_sb = [sb(f"v_sb{i}", [128, S], BF16, cx) for i in range(2)]
                vt_bf = sb("vt_bf", [128, S], BF16, cx)
                s_sb = sb("s_sb", [128, S], F32, cx)
                p_bf = [sb(f"p_bf{i}", [128, S], BF16, cx) for i in range(2)]
                pt_sb = sb("pt_sb", [128, S], BF16, cx)
                alibi_sb = sb("alibi_sb", [128, S], F32, cx)
                atn_h = [sb(f"atn_h{i}", [128, 512], BF16, cx) for i in range(2)]
                bq_c = [sb(f"bq_c{i}", [128, 1], F32, cx) for i in range(2)]
                bk_c = [sb(f"bk_c{i}", [128, 1], F32, cx) for i in range(2)]
                bv_c = [sb(f"bv_c{i}", [128, 1], F32, cx) for i in range(2)]
                bqs_c = [sb(f"bqs_c{i}", [128, 1], F32, cx) for i in range(2)]
                mrow = [sb(f"mrow{i}", [128, 1], F32, cx) for i in range(2)]
                negm = [sb(f"negm{i}", [128, 1], F32, cx) for i in range(2)]
                ssum = [sb(f"ssum{i}", [128, 1], F32, cx) for i in range(2)]
                rr = [sb(f"rr{i}", [128, 1], F32, cx) for i in range(2)]
                ssem = [kb.sem(f"wsl{i}") for i in range(2)]
                bsem = [kb.sem(f"bqkv{i}") for i in range(2)]
                alsem = kb.sem("albi")

                bank_g = {i: 0 for i in range(6)}
                bank_sem = {i: "dve" for i in range(6)}
                pt_g = {0: 0, 1: 0}
                atn_st = {0: 0, 1: 0}
                albi_used_v = [0]
                slab_pe = {0: 0, 1: 0}
                bias_used = {0: 0, 1: 0}
                vt_act = {0: 0, 1: 0}
                qkv_pe_done = {0: 0, 1: 0}
                pnorm_d = {}
                exp_a = {}

                def slab_load(i, sl):
                    h_, part = i // 3, i % 3
                    kb.wait("gpsimd", "pe", slab_pe[sl])
                    kb.dma("gpsimd",
                           wslab[sl][:].rearrange("p (c o) -> p c o", c=KC),
                           wqkvT[:, part * DM + h_ * 128: part * DM + (h_ + 1) * 128]
                           .rearrange("(c p) o -> p c o", p=128), ssem[sl])

                def emit_bias_alibi(h):
                    p = h % 2
                    kb.wait("sync", "act", bias_used[p])
                    for j, lo in ((0, h * 128), (1, DM + h * 128), (2, 2 * DM + h * 128)):
                        dstc = (bq_c, bk_c, bv_c)[j]
                        kb.dma("sync", dstc[p][:],
                               bqkv[0:1, lo:lo + 128].rearrange("a k -> k a"), bsem[p])
                    kb.wait("vector", bsem[p], kb.cnt[bsem[p]])
                    kb.emit("vector", lambda v, p=p: v.tensor_scalar(
                        bqs_c[p][:], bq_c[p][:], Q_SCALE, None, op0=OP.mult), inc="dve")
                    kb.wait("scalar", "dve", kb.cnt["dve"])

                def emit_alibi(h):
                    kb.wait("sync", "dve", albi_used_v[0])
                    kb.dma("sync", alibi_sb[:],
                           alibi_c[h:h + 1, :].partition_broadcast(128).squeeze(1),
                           alsem)

                def emit_qkv_unit(h, unit):
                    p = h % 2
                    i = h * 3
                    if unit < 3:
                        part = unit
                        sl = (i + part) % 2
                        kb.wait("tensor", ssem[sl], kb.cnt[ssem[sl]])
                        for tc in range(4):
                            bank = tc % 2
                            kb.wait("tensor", bank_sem[bank], bank_g[bank])
                            for kc in range(KC):
                                kb.emit("tensor", lambda e, kc=kc, tc=tc, sl=sl,
                                        bank=bank: e.matmul(
                                    pb[bank][:],
                                    wslab[sl][:, kc * 128:(kc + 1) * 128],
                                    h3T[:, kc * S + tc * 512: kc * S + (tc + 1) * 512],
                                    start=(kc == 0), stop=(kc == KC - 1)),
                                    inc=("pe" if kc == KC - 1 else None))
                            kb.wait("scalar", "pe", kb.cnt["pe"])
                            if part == 0:
                                kb.emit("scalar", lambda s, tc=tc, bank=bank, p=p:
                                        s.activation(
                                    q_sb[p][:, tc * 512:(tc + 1) * 512], pb[bank][:],
                                    AF.Identity, bias=bqs_c[p][:], scale=Q_SCALE),
                                    inc="act")
                            elif part == 1:
                                kb.emit("scalar", lambda s, tc=tc, bank=bank, p=p:
                                        s.activation(
                                    k_sb[p][:, tc * 512:(tc + 1) * 512], pb[bank][:],
                                    AF.Identity, bias=bk_c[p][:], scale=1.0), inc="act")
                            else:
                                kb.emit("scalar", lambda s, tc=tc, bank=bank, p=p:
                                        s.activation(
                                    vt_bf[:, tc * 512:(tc + 1) * 512], pb[bank][:],
                                    AF.Identity, bias=bv_c[p][:], scale=1.0), inc="act")
                            bank_g[bank], bank_sem[bank] = kb.cnt["act"], "act"
                        slab_pe[sl] = kb.cnt["pe"]
                        bias_used[p] = kb.cnt["act"]
                        if part == 2:
                            vt_act[p] = kb.cnt["act"]
                        if i + part + 2 < 3 * H:
                            slab_load(i + part + 2, sl)
                    else:
                        g4 = unit - 3
                        bk = g4 % 2
                        kb.wait("tensor", "act", vt_act[p])
                        kb.wait("tensor", "act", pt_g[bk])
                        for j in range(4):
                            kb_i = g4 * 4 + j
                            kb.emit("tensor", lambda e, kb_i=kb_i, j=j, bk=bk:
                                    e.transpose(
                                pTb[bk][:, j * 128:(j + 1) * 128],
                                vt_bf[:, kb_i * 128:(kb_i + 1) * 128], id_sb[:]),
                                inc=("pe" if j == 3 else None))
                        kb.wait("scalar", "pe", kb.cnt["pe"])
                        kb.emit("scalar", lambda s, g4=g4, bk=bk, p=p: s.copy(
                            v_sb[p][:, g4 * 512:(g4 + 1) * 512],
                            pTb[bk][:, 0:512]), inc="act")
                        pt_g[bk] = kb.cnt["act"]

                def emit_S1(h, qi):
                    p, sp = h % 2, qi % 2
                    L = (qi + 1) * 128
                    nch = (L + 511) // 512
                    kb.wait("vector", "act", exp_a.get((h * NT + qi) - 1, 0))
                    for ch in range(nch):
                        w = min(512, L - ch * 512)
                        bank = 4 + ch % 2
                        kb.wait("tensor", bank_sem[bank], bank_g[bank])
                        kb.emit("tensor", lambda e, qi=qi, ch=ch, w=w, bank=bank, p=p:
                                e.matmul(
                            pb[bank][:, 0:w],
                            q_sb[p][:, qi * 128:(qi + 1) * 128],
                            k_sb[p][:, ch * 512: ch * 512 + w],
                            start=True, stop=True), inc="pe")
                        kb.wait("vector", "pe", kb.cnt["pe"])
                        kb.wait("vector", alsem, kb.cnt[alsem])
                        kb.emit("vector", lambda v, ch=ch, w=w, bank=bank:
                                v.tensor_tensor(
                            s_sb[:, ch * 512: ch * 512 + w], pb[bank][:, 0:w],
                            alibi_sb[:, ch * 512: ch * 512 + w], op=OP.add),
                            inc="dve")
                        bank_g[bank], bank_sem[bank] = kb.cnt["dve"], "dve"
                        albi_used_v[0] = kb.cnt["dve"]
                    kb.emit("vector", lambda v, L=L: v.tensor_tensor(
                        s_sb[:, L - 128:L], s_sb[:, L - 128:L], causal_sb[:],
                        op=OP.add), inc="dve")
                    kb.wait("vector", "dve", kb.cnt["dve"])
                    kb.emit("vector", lambda v, L=L, sp=sp: v.tensor_reduce(
                        mrow[sp][:], s_sb[:, 0:L], axis=AX.X, op=OP.max),
                        inc="dve")
                    kb.wait("vector", "dve", kb.cnt["dve"])
                    kb.emit("vector", lambda v, sp=sp: v.tensor_scalar(
                        negm[sp][:], mrow[sp][:], -1.0, None, op0=OP.mult), inc="dve")
                    kb.wait("scalar", "dve", kb.cnt["dve"])
                    kb.emit("scalar", lambda s, L=L, sp=sp: s.activation(
                        p_bf[sp][:, 0:L], s_sb[:, 0:L], AF.Exp, bias=negm[sp][:],
                        scale=1.0, accum_out=ssum[sp][:]), inc="act")
                    exp_a[h * NT + qi] = kb.cnt["act"]

                def emit_S1fin(h, qi):
                    sp = qi % 2
                    L = (qi + 1) * 128
                    kb.wait("vector", "act", exp_a[h * NT + qi])
                    kb.emit("vector", lambda v, sp=sp: v.reciprocal(
                        rr[sp][:], ssum[sp][:]), inc="dve")
                    kb.wait("vector", "dve", kb.cnt["dve"])
                    kb.emit("vector", lambda v, L=L, sp=sp: v.tensor_scalar(
                        p_bf[sp][:, 0:L], p_bf[sp][:, 0:L], rr[sp][:], None,
                        op0=OP.mult), inc="dve")
                    pnorm_d[qi] = kb.cnt["dve"]

                def emit_S2(h, qi):
                    p, sp = h % 2, qi % 2
                    kb.wait("tensor", "dve", pnorm_d[qi])
                    for kb_i in range(qi + 1):
                        bk = (kb_i // 4) % 2
                        if kb_i % 4 == 0:
                            kb.wait("tensor", "act", pt_g[bk])
                        last = (kb_i % 4 == 3) or (kb_i == qi)
                        kb.emit("tensor", lambda e, kb_i=kb_i, bk=bk, sp=sp:
                                e.transpose(
                            pTb[bk][:, (kb_i % 4) * 128:((kb_i % 4) + 1) * 128],
                            p_bf[sp][:, kb_i * 128:(kb_i + 1) * 128], id_sb[:]),
                            inc=("pe" if last else None))
                        if last:
                            lo = (kb_i // 4) * 512
                            n = (kb_i % 4 + 1) * 128
                            kb.wait("scalar", "pe", kb.cnt["pe"])
                            kb.emit("scalar", lambda s, lo=lo, n=n, bk=bk: s.copy(
                                pt_sb[:, lo:lo + n], pTb[bk][:, 0:n]), inc="act")
                            pt_g[bk] = kb.cnt["act"]
                    PT_ACT = kb.cnt["act"]
                    av = 2 + qi % 2
                    kb.wait("tensor", bank_sem[av], bank_g[av])
                    kb.wait("tensor", "act", PT_ACT)
                    for kb_i in range(qi + 1):
                        kb.emit("tensor", lambda e, kb_i=kb_i, av=av, qi=qi, p=p:
                                e.matmul(
                            pb[av][:, 0:128],
                            v_sb[p][:, kb_i * 128:(kb_i + 1) * 128],
                            pt_sb[:, kb_i * 128:(kb_i + 1) * 128],
                            start=(kb_i == 0), stop=(kb_i == qi)),
                            inc=("pe" if kb_i == qi else None))
                    slot = (qi // 4) % 2
                    kb.wait("scalar", "pe", kb.cnt["pe"])
                    kb.wait("scalar", "qst", atn_st[slot])
                    kb.emit("scalar", lambda s, qi=qi, av=av, slot=slot: s.copy(
                        atn_h[slot][:, (qi % 4) * 128:((qi % 4) + 1) * 128],
                        pb[av][:, 0:128]), inc="act")
                    bank_g[av], bank_sem[av] = kb.cnt["act"], "act"
                    if qi % 4 == 3:
                        kb.wait("sync", "act", kb.cnt["act"])
                        atn_st[slot] = kb.dma(
                            "sync", attnT_sc[h][:, (qi - 3) * 128:(qi + 1) * 128],
                            atn_h[slot][:], "qst")

                slab_load(0, 0)
                slab_load(1, 1)
                emit_bias_alibi(0)
                emit_alibi(0)
                for u in range(7):
                    emit_qkv_unit(0, u)
                for h in range(H):
                    if h + 1 < H:
                        emit_bias_alibi(h + 1)
                    for qi in range(NT):
                        emit_S1(h, qi)
                        if qi == NT - 1 and h + 1 < H:
                            emit_alibi(h + 1)
                        if qi >= 1:
                            emit_S1fin(h, qi - 1)
                            emit_S2(h, qi - 1)
                        if h + 1 < H and qi < 7:
                            emit_qkv_unit(h + 1, qi)
                    emit_S1fin(h, NT - 1)
                    emit_S2(h, NT - 1)
                kb.wait("sync", "qst", kb.cnt["qst"])
            kb.barrier()


            dstC = y if stage == 'C' else x3_sc
            with ExitStack() as cx:
                atn_all = sb("atn_all", [128, KC * S], BF16, cx)
                woslab = [sb(f"wo{i}", [128, KC * 512], BF16, cx) for i in range(2)]
                bout_bc = [sb(f"bout{i}", [128, 512], F32, cx) for i in range(2)]
                xres = [sb(f"c3xr{i}", [128, 512], F32, cx) for i in range(2)]
                xout = [sb(f"c3xo{i}", [128, 512], F32, cx) for i in range(2)]
                asem = kb.sem("c3a")
                wsem2 = [kb.sem(f"c3w{i}") for i in range(2)]
                rsem2 = [kb.sem(f"c3r{i}") for i in range(2)]

                bsem3 = [kb.sem(f"c3b{i}") for i in range(2)]
                bb_used = {0: 0, 1: 0}
                kb.dma("sync", bout_bc[0][:],
                       bout[0:1, 0:512].partition_broadcast(128).squeeze(1), bsem3[0])
                kb.dma("sync", bout_bc[1][:],
                       bout[0:1, 512:1024].partition_broadcast(128).squeeze(1),
                       bsem3[1])
                kb.dma("sync", atn_all[:].rearrange("p (c f) -> p c f", c=KC),
                       attnT_sc.rearrange("c k f -> k c f"), asem)

                def wo_load(fc, sl):
                    kb.dma("gpsimd",
                           woslab[sl][:].rearrange("p (c o) -> p c o", c=KC),
                           woutT[:, fc * 512:(fc + 1) * 512]
                           .rearrange("(c p) o -> p c o", p=128), wsem2[sl])

                wo_load(0, 0)
                wo_load(1, 1)
                kb.wait("tensor", asem, kb.cnt[asem])
                bank_g = {i: 0 for i in range(4)}
                slab_pe2 = {0: 0, 1: 0}
                x_used = {i: 0 for i in range(2)}
                xo_st = {i: 0 for i in range(2)}
                xi = 0
                for fc in range(8):
                    sl = fc % 2
                    kb.wait("tensor", wsem2[sl], kb.cnt[wsem2[sl]])
                    for t in range(NT):
                        bank = t % 4
                        xs = xi % 2
                        kb.wait("sync", "dve", x_used[xs])
                        kb.dma("sync", xres[xs][:],
                               x2_sc[t * 128:(t + 1) * 128, fc * 512:(fc + 1) * 512],
                               rsem2[xs])
                        kb.wait("tensor", "dve", bank_g[bank])
                        for kc in range(KC):
                            kb.emit("tensor", lambda e, kc=kc, t=t, sl=sl, bank=bank:
                                    e.matmul(
                                pb[bank][:],
                                atn_all[:, kc * S + t * 128: kc * S + (t + 1) * 128],
                                woslab[sl][:, kc * 512:(kc + 1) * 512],
                                start=(kc == 0), stop=(kc == KC - 1)),
                                inc=("pe" if kc == KC - 1 else None))
                        slab_pe2[sl] = kb.cnt["pe"]
                        kb.wait("vector", "pe", kb.cnt["pe"])
                        kb.wait("vector", rsem2[xs], kb.cnt[rsem2[xs]])
                        kb.wait("vector", "qst", xo_st[xs])
                        if t == 0:
                            kb.wait("vector", bsem3[sl], kb.cnt[bsem3[sl]])
                        kb.emit("vector", lambda v, bank=bank, sl=sl, xs=xs:
                                v.scalar_tensor_tensor(
                                    xout[xs][:], pb[bank][:], 1.0,
                                    bout_bc[sl][:],
                                    op0=OP.mult, op1=OP.add))
                        kb.emit("vector", lambda v, xs=xs: v.tensor_tensor(
                            xout[xs][:], xout[xs][:], xres[xs][:], op=OP.add),
                            inc="dve")
                        bank_g[bank] = kb.cnt["dve"]
                        x_used[xs] = kb.cnt["dve"]
                        bb_used[sl] = kb.cnt["dve"]
                        kb.wait("sync", "dve", kb.cnt["dve"])
                        xo_st[xs] = kb.dma(
                            "sync", dstC[t * 128:(t + 1) * 128,
                                         fc * 512:(fc + 1) * 512], xout[xs][:], "qst")
                        xi += 1
                    if fc + 2 < 8:
                        kb.wait("sync", "dve", bb_used[sl])
                        kb.dma("sync", bout_bc[sl][:],
                               bout[0:1, (fc + 2) * 512:(fc + 3) * 512]
                               .partition_broadcast(128).squeeze(1), bsem3[sl])
                        kb.wait("gpsimd", "pe", slab_pe2[sl])
                        wo_load(fc + 2, sl)
                kb.wait("sync", "qst", kb.cnt["qst"])
            kb.barrier()

        if lvl >= 6:
            ln_phase("l4", x3_sc, dst_dram=hT_sc)
            ffn_phase("f2", x3_sc, hT_sc, y)

        kb.wait("sync", "qst", kb.cnt["qst"])
        kb.wait("gpsimd", "qw", kb.cnt["qw"])
        kb.barrier()

        block = ctx.enter_context(nc.Block())
        for eng in KB.ENGS:
            def make(eng):
                def body(e):
                    for fn in kb.q[eng]:
                        fn(e)
                return body
            getattr(block, eng)(make(eng))

    return nc


def _prep_inputs(inputs):
    f32 = np.float32
    bf16 = ml_dtypes.bfloat16
    x = np.ascontiguousarray(inputs["L0_attn_a_data"], dtype=f32)
    mask = np.ascontiguousarray(inputs["L0_attn_a_mask"], dtype=np.int32)
    shared = {
        "mask_i": mask,
        "waT": np.ascontiguousarray(np.asarray(inputs["L0_attn_a_weight"], dtype=f32).T),
        "bias_a": np.asarray(inputs["L0_attn_a_bias"], dtype=f32).reshape(1, DM),
        "wqkvT": np.ascontiguousarray(np.asarray(inputs["L0_attn_b_w_qkv"], dtype=f32).T),
        "bqkv": np.asarray(inputs["L0_attn_b_b_qkv"], dtype=f32).reshape(1, 3 * DM),
        "woutT": np.ascontiguousarray(np.asarray(inputs["L0_attn_b_w_out"], dtype=f32).T),
        "bout": np.asarray(inputs["L0_attn_b_b_out"], dtype=f32).reshape(1, DM),
        "AT": np.ascontiguousarray(np.asarray(inputs["L0_ffn_A_w"], dtype=f32).T),
        "BT": np.ascontiguousarray(np.asarray(inputs["L0_ffn_B_w"], dtype=f32).T),
        "WqT": np.ascontiguousarray(np.asarray(inputs["L0_ffn_Wq_w"], dtype=f32).T),
        "keysT": np.ascontiguousarray(np.asarray(inputs["L0_ffn_keys"], dtype=f32).T),
        "id_bf": np.eye(128, dtype=f32).astype(bf16),
        "id_f": np.eye(128, dtype=f32),
        "causal_c": np.where(np.arange(128)[:, None] >= np.arange(128)[None, :],
                             0.0, NEGBIG).astype(f32),
        "pos_c": np.ascontiguousarray(
            np.broadcast_to((np.arange(S) - (S - 1)).astype(f32), (128, S))),
        "alibi_c": np.ascontiguousarray(
            np.array(SLOPES, dtype=f32)[:, None] *
            (np.arange(S) - (S - 1)).astype(f32)[None, :]),
        "blk_ind": (np.arange(DM)[None, :] // 32 ==
                    np.arange(128)[:, None]).astype(bf16),
        "blk_indT": np.ascontiguousarray(
            (np.arange(DM)[:, None] // 32 == np.arange(128)[None, :]).astype(bf16)),
        "repE": (np.arange(BOT)[None, :] // RANK ==
                 np.arange(NE)[:, None]).astype(bf16),
    }
    in_maps = []
    for c in range(8):
        m = dict(shared)
        m["x0"] = np.ascontiguousarray(x[c])
        in_maps.append(m)
    return in_maps


def _run(inputs, stage_=None, debug=()):
    st = stage_ or STAGE
    key = (st, tuple(sorted(debug)))
    if key not in _CACHE:
        _CACHE[key] = build_program(st, frozenset(debug))
    nc = _CACHE[key]
    in_maps = _prep_inputs(inputs)
    return run_bass_kernel_spmd(nc, in_maps, core_ids=list(range(8)))


def kernel(**inputs):
    res = _run(inputs)
    return np.stack([r["y"] for r in res.results], axis=0).astype(np.float32)
